# version 03
# baseline (speedup 1.0000x reference)
import contextlib
import math
import os

KDBG = set(os.environ.get("KDBG", "").split(","))
import numpy as np
import ml_dtypes
import concourse.bass as bass
import concourse.mybir as mybir
from concourse.bass_utils import run_bass_kernel_spmd

F32 = mybir.dt.float32
BF16 = mybir.dt.bfloat16
F16 = mybir.dt.float16
I32 = mybir.dt.int32
U8 = mybir.dt.uint8
AF = mybir.ActivationFunctionType
ALU = mybir.AluOpType
AXL = mybir.AxisListType

D = 2048
T = 8192
NCORES = 8
SW = 512
NSTRIP = 16
TLOC = 2 * SW
FC = D // 128
BF = ml_dtypes.bfloat16
I_FFN = 5632
IC_FFN = I_FFN // 128


def strips_of(core):
    return (core, NSTRIP - 1 - core)



def tile_lhsT(WT):
    K, M = WT.shape
    assert K % 128 == 0 and M % 128 == 0
    return np.ascontiguousarray(
        WT.reshape(K // 128, 128, M // 128, 128).transpose(2, 1, 0, 3))


def col_layout(v):
    return np.ascontiguousarray(np.asarray(v).reshape(-1, 128).T)


def ap3(t, off, dims):
    return bass.AP(t.tensor if isinstance(t, bass.AP) else t, off, dims)



ENGS = ("tensor", "vector", "scalar", "gpsimd", "sync")


class Sem:
    __slots__ = ("h", "n", "name")

    def __init__(self, h, name):
        self.h = h
        self.n = 0
        self.name = name


class Arena:
    def __init__(self, bld, name, nbytes):
        self.t = bld.nc_sbuf(name, [128, nbytes], U8)
        self.nbytes = nbytes
        self.off = 0
        self.peak = 0

    def alloc(self, rows, cols, dtype):
        esz = mybir.dt.size(dtype)
        sz = (esz * cols + 31) & ~31
        assert self.off + sz <= self.nbytes, \
            f"arena {self.t.name} overflow: {self.off}+{sz} > {self.nbytes}"
        ap = self.t[0:rows, self.off:self.off + esz * cols].bitcast(dtype)
        self.off += sz
        self.peak = max(self.peak, self.off)
        return ap

    def mark(self):
        return self.off

    def reset(self, to=0):
        self.peak = max(self.peak, self.off)
        self.off = to


class DmaSlot:

    def __init__(self, bld, name, eng):
        self.bld = bld
        self.sem = bld.sem("q_" + name)
        self.eng = eng

    def fill(self, dst, src, waits=()):
        self.bld.dma(self.eng, dst, src, waits=waits, inc=self.sem)
        return (self.sem, self.sem.n)


class B:
    def __init__(self, nc):
        self.nc = nc
        self.es = contextlib.ExitStack()
        self.streams = {e: [] for e in ENGS}
        self._waited = {e: {} for e in ENGS}
        self.sems = {}
        self.psum = []
        self.bank_guard = {}

    def nc_sbuf(self, name, shape, dtype):
        return self.es.enter_context(self.nc.sbuf_tensor(name, shape, dtype))

    def nc_psum(self, name, shape, dtype):
        return self.es.enter_context(self.nc.psum_tensor(name, shape, dtype))

    def sem(self, name):
        if name in self.sems:
            return self.sems[name]
        h = self.es.enter_context(self.nc.semaphore(name))
        s = Sem(h, name)
        self.sems[name] = s
        return s

    def alloc_psum_banks(self):
        for i in range(8):
            self.psum.append(self.nc_psum(f"psbank{i}", [128, 512], F32))

    SERIAL_ENGS = ("vector", "scalar", "gpsimd")

    def eng_sem(self, eng):
        return self.sem({"vector": "dv", "scalar": "ac", "gpsimd": "gp"}[eng])

    def emit(self, eng, fn, waits=(), inc=None, amt=1, serial=False):
        if serial and eng in self.SERIAL_ENGS:
            es = self.eng_sem(eng)
            waits = list(waits) + [(es, es.n)]
            assert inc is None or inc is es, \
                f"serial op on {eng} must inc engine sem {es.name}"
            inc = es
            amt = 1
        wl = []
        seen = self._waited[eng]
        for s, v in waits:
            if v <= 0 or seen.get(s.name, -1) >= v:
                continue
            seen[s.name] = v
            wl.append((s.h, v))
        tgt = None
        if inc is not None:
            inc.n += amt
            tgt = (inc.h, amt)
        if not wl and fn is None and tgt is None:
            return

        def run(e):
            for h, v in wl:
                e.wait_ge(h, v)
            r = fn(e) if fn is not None else None
            if tgt is not None:
                if r is None:
                    r = e.nop()
                r.then_inc(tgt[0], tgt[1])
        self.streams[eng].append(run)

    def mm(self, out, lhsT, rhs, start, stop, waits=(), inc=None, tp=None):
        if tp is None:
            self.emit("tensor",
                      lambda e, o=out, l=lhsT, r=rhs, s=start, p=stop:
                      e.matmul(o, l, r, start=s, stop=p), waits=waits, inc=inc)
        else:
            self.emit("tensor",
                      lambda e, o=out, l=lhsT, r=rhs, s=start, p=stop, t=tp:
                      e.matmul(o, l, r, start=s, stop=p, tile_position=t),
                      waits=waits, inc=inc)

    def transp(self, out_ps, in_ap, ident, waits=(), inc=None):
        self.emit("tensor",
                  lambda e, o=out_ps, i=in_ap, d=ident: e.transpose(o, i, d),
                  waits=waits, inc=inc)

    def tt(self, eng, out, a, b, op, waits=(), inc=None):
        self.emit(eng, lambda e, o=out, a=a, b=b, op=op: e.tensor_tensor(o, a, b, op),
                  waits=waits, inc=inc, serial=True)

    def ts(self, eng, out, a, s1, s2, op0, op1, waits=(), inc=None):
        self.emit(eng, lambda e, o=out, a=a, s1=s1, s2=s2, op0=op0, op1=op1:
                  e.tensor_scalar(o, a, s1, s2, op0, op1), waits=waits, inc=inc, serial=True)

    def tss(self, eng, out, a, s, op, waits=(), inc=None):
        self.emit(eng, lambda e, o=out, a=a, s=s, op=op:
                  e.tensor_single_scalar(o, a, s, op), waits=waits, inc=inc, serial=True)

    def stt(self, eng, out, in0, scalar, in1, op0, op1, waits=(), inc=None):
        self.emit(eng, lambda e, o=out, i0=in0, s=scalar, i1=in1, op0=op0, op1=op1:
                  e.scalar_tensor_tensor(o, i0, s, i1, op0, op1), waits=waits, inc=inc, serial=True)

    def act(self, out, a, func, bias=None, scale=None, accum_out=None, waits=(), inc=None):
        kw = {}
        if bias is not None:
            kw["bias"] = bias
        if scale is not None:
            kw["scale"] = scale
        if accum_out is not None:
            kw["accum_out"] = accum_out
        self.emit("scalar", lambda e, o=out, a=a, f=func, kw=kw: e.activation(o, a, f, **kw),
                  waits=waits, inc=inc, serial=True)

    def cp(self, eng, out, a, waits=(), inc=None):
        self.emit(eng, lambda e, o=out, a=a: e.tensor_copy(o, a), waits=waits, inc=inc, serial=True)

    def red(self, out, a, op, absval=None, waits=(), inc=None):
        self.emit("vector", lambda e, o=out, a=a, op=op, ab=absval:
                  e.tensor_reduce(o, a, axis=AXL.X, op=op, apply_absolute_value=ab),
                  waits=waits, inc=inc, serial=True)

    def scan(self, out, d0, d1, init, op0, op1, waits=(), inc=None):
        self.emit("vector", lambda e, o=out, d0=d0, d1=d1, i=init, op0=op0, op1=op1:
                  e.tensor_tensor_scan(o, d0, d1, i, op0, op1), waits=waits, inc=inc, serial=True)

    def sel(self, out, mask, t, f, waits=(), inc=None):
        self.emit("vector", lambda e, o=out, m=mask, t=t, f=f: e.select(o, m, t, f),
                  waits=waits, inc=inc, serial=True)

    def recip(self, out, a, waits=(), inc=None):
        self.emit("vector", lambda e, o=out, a=a: e.reciprocal(o, a), waits=waits, inc=inc, serial=True)

    def dma(self, eng, out, in_, waits=(), inc=None):
        self.emit(eng, lambda e, o=out, i=in_: e.dma_start(out=o, in_=i),
                  waits=waits, inc=inc, amt=16)

    def mset(self, eng, ap, v, waits=(), inc=None):
        self.emit(eng, lambda e, a=ap, v=v: e.memset(a, v), waits=waits, inc=inc, serial=True)

    def dma_slot(self, name, eng="sync"):
        return DmaSlot(self, name, eng)

    def use_bank(self, idx, extra_waits=()):
        w = list(extra_waits)
        w.extend(self.bank_guard.get(idx, ()))
        return tuple(w)

    def guard_bank(self, idx, sem, val, add=False):
        if add and idx in self.bank_guard:
            self.bank_guard[idx].append((sem, val))
        else:
            self.bank_guard[idx] = [(sem, val)]

    def wait_engines(self, waits, engines=("vector", "gpsimd", "scalar")):
        for e in engines:
            self.emit(e, None, waits=tuple(waits))

    def join_all(self, names=None):
        targets = [(s, s.n) for nm, s in self.sems.items()
                   if (names is None or nm in names) and s.n > 0]
        for e in ENGS:
            self.emit(e, None, waits=tuple(targets))

    def run_block(self):
        with self.nc.Block() as block:
            for eng in ENGS:
                stream = self.streams[eng]

                def make(stream):
                    def body(e):
                        for fn in stream:
                            fn(e)
                    return body
                getattr(block, eng)(make(stream))



class Common:
    def __init__(self, bld, hb):
        self.bld = bld
        self.ones_col_bf = bld.nc_sbuf("ones_col_bf", [128, 1], BF16)
        self.ones_row_bf = bld.nc_sbuf("ones_row_bf", [1, 128], BF16)
        self.ones_col_f = bld.nc_sbuf("ones_col_f", [128, 1], F32)
        self.ones_row_f = bld.nc_sbuf("ones_row_f", [1, 128], F32)
        self.ident_bf = bld.nc_sbuf("ident_bf", [128, 128], BF16)
        self.ident_f32 = bld.nc_sbuf("ident_f32", [128, 128], F32)
        sq = bld.sem("q_const")
        for dst, src in [(self.ones_col_bf[:], "c_ones_col"),
                         (self.ones_row_bf[:], "c_ones_row"),
                         (self.ones_col_f[:], "c_ones_col_f"),
                         (self.ones_row_f[:], "c_ones_row_f"),
                         (self.ident_bf[:], "c_ident"),
                         (self.ident_f32[:], "c_ident_f")]:
            bld.dma("sync", dst, hb[src][:, :], inc=sq)
        self.ready = (sq, sq.n)
        self._consts = {}
        self._cpool = bld.nc_sbuf("constpool", [128, 64], F32)
        self._cn = 0

    def constf(self, val):
        val = float(val)
        if val not in self._consts:
            assert self._cn < 64
            ap = self._cpool[:, self._cn:self._cn + 1]
            self._cn += 1
            gp = self.bld.eng_sem("gpsimd")
            self.bld.mset("gpsimd", ap, val)
            self._consts[val] = (ap, (gp, gp.n))
        return self._consts[val]



def ln_fmajor(bld, cm, ar, x_aps, xn_tiles, w, S, x_ready=(), eps=1e-5,
              banks=(6, 7)):
    mm, dv, ac = S["mm"], S["dv"], S["ac"]
    bA, bB = banks
    nt2 = (w + 511) // 512
    assert nt2 == 1, "ln_fmajor now assumes w <= 512"
    y2 = [ar.alloc(128, w, BF16) for _ in range(2)]
    sq_mm = {}
    for c in range(FC):
        last = (c == FC - 1)
        bld.act(y2[c % 2], x_aps[c], AF.Square,
                waits=tuple(x_ready) + ((mm, sq_mm.get(c - 2, 0)),), inc=ac)
        ac_sq = ac.n
        w0 = bld.use_bank(bA, (cm.ready,) + tuple(x_ready)) if c == 0 else ()
        bld.mm(bld.psum[bA][0:1, 0:w], cm.ones_col_f[:],
               x_aps[c][:, 0:w], c == 0, last, waits=w0,
               inc=mm if last else None)
        w1 = bld.use_bank(bB, ((ac, ac_sq),)) if c == 0 else ((ac, ac_sq),)
        bld.mm(bld.psum[bB][0:1, 0:w], cm.ones_col_bf[:],
               y2[c % 2][:, 0:w], c == 0, last, waits=w1, inc=mm)
        sq_mm[c] = mm.n
    stats_done = mm.n
    rows = ar.alloc(1, 4 * w, F32)
    mu, var, rstd, nmur = rows[:, 0:w], rows[:, w:2 * w], rows[:, 2 * w:3 * w], rows[:, 3 * w:4 * w]
    bld.tss("vector", mu, bld.psum[bA][0:1, 0:w], 1.0 / D, ALU.mult,
            waits=((mm, stats_done),), inc=dv)
    bld.tss("vector", var, bld.psum[bB][0:1, 0:w], 1.0 / D, ALU.mult, inc=dv)
    bld.tt("vector", nmur, mu, mu, ALU.mult, inc=dv)
    bld.tt("vector", var, var, nmur, ALU.subtract, inc=dv)
    dv_var = dv.n
    eps_ap, eps_rdy = cm.constf(eps)
    bld.act(rstd, var, AF.Ln, bias=eps_ap[0:1, :], waits=((dv, dv_var), eps_rdy), inc=ac)
    bld.act(rstd, rstd, AF.Exp, scale=-0.5, inc=ac)
    ac_rstd = ac.n
    bld.tt("vector", nmur, mu, rstd, ALU.mult, waits=((ac, ac_rstd),), inc=dv)
    bld.tss("vector", nmur, nmur, -1.0, ALU.mult, inc=dv)
    dv_rows = dv.n
    bld.mm(bld.psum[bA][:, 0:w], cm.ones_row_f[:], rstd[:, 0:w], True, True,
           waits=((dv, dv_rows),), inc=mm)
    bld.mm(bld.psum[bB][:, 0:w], cm.ones_row_f[:], nmur[:, 0:w], True, True, inc=mm)
    bc_done = mm.n
    tmp = ar.alloc(128, w, F32)
    for c in range(FC):
        bld.tt("vector", tmp, x_aps[c], bld.psum[bA][:, 0:w], ALU.mult,
               waits=((mm, bc_done),), inc=dv)
        bld.tt("vector", xn_tiles[c], tmp, bld.psum[bB][:, 0:w], ALU.add, inc=dv)
    done = dv.n
    bld.guard_bank(bA, dv, done)
    bld.guard_bank(bB, dv, done)
    return (dv, done)



def proj_stream(bld, name, w_hbm, MC, KC, rhs_fn, widths, epilogue,
                S, wq, wbufs, rhs_ready=(), banks=(0, 1, 2, 3), mc_list=None,
                kc_base=0, extra_mm_waits=()):
    mm = S["mm"]
    if mc_list is None:
        mc_list = list(range(MC))
    cast = w_hbm.dtype == F32
    qeng = "gpsimd" if cast else "sync"
    mm_after_mc = {}
    kcw = KC * 128
    for i, mc in enumerate(mc_list):
        buf = wbufs[i % 2]
        dep = mm_after_mc.get(i - 2, 0)
        src = ap3(w_hbm, mc * 128 * kcw, [[kcw, 128], [1, kcw]])
        bld.emit(qeng, lambda e, buf=buf, src=src: e.dma_start(out=buf, in_=src),
                 waits=((mm, dep),), inc=wq, amt=16)
        wq_val = wq.n
        nb = len(banks)
        for j, w in enumerate(widths):
            bank = banks[(i * len(widths) + j) % nb]
            ps = bld.psum[bank][:, 0:w]
            for kc in range(KC):
                waits = []
                if kc == 0:
                    waits = list(bld.use_bank(bank, ()))
                    waits.append((wq, wq_val))
                    waits.extend(rhs_ready)
                    waits.extend(extra_mm_waits)
                bld.emit("tensor", lambda e, buf=buf, kc=kc, ps=ps, j=j, w=w: e.matmul(
                    ps, buf[:, kc * 128:(kc + 1) * 128], rhs_fn(kc + kc_base, j),
                    start=(kc == 0), stop=(kc == KC - 1)),
                    waits=tuple(waits))
            bld.emit("tensor", None, inc=mm)
            mmv = mm.n
            rel = epilogue(mc, j, ps, (mm, mmv))
            bld.guard_bank(bank, rel[0], rel[1])
        mm_after_mc[i] = mm.n



DEQ_GU_PER_CORE = 11
DEQ_DN_PER_CORE = 2


def emit_dequant(bld, cm, ar, S, hb):
    dv, gp, qg, qs = S["dv"], S["gp"], S["qg"], S["qs"]
    ints = ar.alloc(128, 4, I32)
    bld.dma("sync", ints, hb["c_ints"][:, :], inc=qs)
    ints_rdy = (qs, qs.n)
    c_sh, c8, c7, c11 = (ints[:, i:i + 1] for i in range(4))

    for tag, packedT, scales, KC, n_loc in (
            ("gu", hb["p_gu"], hb["s_gu"], 16, DEQ_GU_PER_CORE),
            ("dn", hb["p_dn"], hb["s_dn"], 44, DEQ_DN_PER_CORE)):
        n_o = n_loc * 128
        slab = [ar.alloc(128, KC * 128, BF16) for _ in range(n_loc)]
        wtile = [ar.alloc(128, 128, I32) for _ in range(2)]
        stile = [ar.alloc(128, 128, F16) for _ in range(2)]
        slot_w = [bld.dma_slot(f"dq{tag}w{i}", "gpsimd") for i in range(2)]
        slot_s = [bld.dma_slot(f"dq{tag}s{i}", "gpsimd") for i in range(2)]
        u = ar.alloc(128, 128, I32)
        s8 = ar.alloc(128, 128, I32)
        n7 = ar.alloc(128, 128, I32)
        fa = ar.alloc(128, 128, F32)
        uf = ar.alloc(128, 128, F32)
        a1i = ar.alloc(128, 128, I32)
        bits = ar.alloc(128, 128, I32)
        dc = ar.alloc(128, 128, F32)
        sg = ar.alloc(128, 128, F32)
        vv = ar.alloc(128, 128, F32)
        decode_end = []
        ti = 0
        for li in range(n_loc):
            for kc in range(KC):
                wt = wtile[ti % 2]
                st = stile[ti % 2]
                src = ap3(packedT, (16 * kc) * n_o + li * 128,
                          [[n_o, 16], [0, 8], [1, 128]])
                ssrc = ap3(scales, (128 * kc) * n_o + li * 128,
                           [[n_o, 128], [1, 128]])
                dep = decode_end[ti - 2] if ti >= 2 else 0
                qv_w = slot_w[ti % 2].fill(wt, src, waits=((dv, dep),))
                qv_s = slot_s[ti % 2].fill(st, ssrc)
                bld.ts("vector", u, wt, c_sh, 15, ALU.logical_shift_right, ALU.bitwise_and,
                       waits=(qv_w, qv_s, ints_rdy))
                bld.ts("vector", s8, u, c8, c8, ALU.bitwise_and, ALU.bitwise_and)
                bld.ts("vector", n7, u, c7, c7, ALU.bitwise_and, ALU.bitwise_and)
                bld.ts("vector", fa, s8, 63.0, 252.0, ALU.mult, ALU.add)
                bld.cp("vector", uf, u)
                bld.tt("vector", fa, fa, uf, ALU.add)
                bld.cp("vector", a1i, fa)
                bld.ts("vector", bits, a1i, c11, c11, ALU.logical_shift_left,
                       ALU.logical_shift_left)
                bld.ts("vector", dc, n7, -0.25, 0.5, ALU.mult, ALU.add)
                bld.ts("vector", dc, dc, 1.0, 0.0, ALU.mult, ALU.max)
                bld.ts("vector", sg, s8, -0.25, 1.0, ALU.mult, ALU.add)
                bld.tt("vector", dc, dc, sg, ALU.mult)
                bld.tt("vector", vv, bits.bitcast(F32), dc, ALU.subtract)
                bld.tt("vector", slab[li][:, kc * 128:(kc + 1) * 128], vv, st, ALU.mult)
                decode_end.append(dv.n)
                ti += 1
        out_t = hb[f"deq_{tag}_loc"]
        dvv = dv.n
        for li in range(n_loc):
            dst = ap3(out_t, li * 128 * KC * 128, [[KC * 128, 128], [1, KC * 128]])
            bld.dma("sync", dst, slab[li], waits=((dv, dvv),), inc=qs)



def emit_l0ffn(bld, cm, ar, S, hb, x_tiles, x_ready, stop_after="full"):
    mm, dv, ac, qw = S["mm"], S["dv"], S["ac"], S["qw"]
    deq_gu, deq_dn = hb["deq_gu"], hb["deq_dn"]
    done = x_ready
    mark0 = ar.mark()
    for h in range(2):
        sl = slice(h * 512, (h + 1) * 512)
        htile = [ar.alloc(128, 512, BF16) for _ in range(IC_FFN)]
        mark1 = ar.mark()
        xn = [ar.alloc(128, 512, BF16) for _ in range(FC)]
        xs = [x_tiles[c][:, sl] for c in range(FC)]
        ln_done = ln_fmajor(bld, cm, ar, xs, xn, 512, S, x_ready=(done,))
        sgt = [ar.alloc(128, 512, BF16) for _ in range(2)]
        wg = [ar.alloc(128, 2048, BF16) for _ in range(2)]
        wu = [ar.alloc(128, 2048, BF16) for _ in range(2)]
        slot_g = [bld.dma_slot(f"l0g{i}_h{h}") for i in range(2)]
        slot_u = [bld.dma_slot(f"l0u{i}_h{h}") for i in range(2)]
        mm_after_pair = {}
        sg_guards = {}
        for j in range(IC_FFN):
            bg, bu = wg[j % 2], wu[j % 2]
            dep = mm_after_pair.get(j - 2, 0)
            qv_g = slot_g[j % 2].fill(bg, ap3(deq_gu, j * 128 * 2048, [[2048, 128], [1, 2048]]),
                                      waits=((mm, dep), done))
            qv_u = slot_u[j % 2].fill(bu, ap3(deq_gu, (44 + j) * 128 * 2048, [[2048, 128], [1, 2048]]),
                                      waits=(done,))
            bank_g, bank_u = (2 * j) % 4, (2 * j + 1) % 4
            for kc in range(FC):
                w0 = bld.use_bank(bank_g, (qv_g, ln_done)) if kc == 0 else ()
                bld.mm(bld.psum[bank_g][:, 0:512], bg[:, kc * 128:(kc + 1) * 128], xn[kc],
                       kc == 0, kc == FC - 1, waits=w0,
                       inc=mm if kc == FC - 1 else None)
            mm_g = mm.n
            for kc in range(FC):
                w0 = bld.use_bank(bank_u, (qv_u,)) if kc == 0 else ()
                bld.mm(bld.psum[bank_u][:, 0:512], bu[:, kc * 128:(kc + 1) * 128], xn[kc],
                       kc == 0, kc == FC - 1, waits=w0,
                       inc=mm if kc == FC - 1 else None)
            mm_u = mm.n
            mm_after_pair[j] = mm.n
            sg = sgt[j % 2]
            sg_guard = sg_guards.get(j % 2, 0)
            bld.act(sg, bld.psum[bank_g][:, 0:512], AF.Silu,
                    waits=((mm, mm_g), (dv, sg_guard)), inc=ac)
            bld.guard_bank(bank_g, ac, ac.n)
            bld.tt("vector", htile[j], bld.psum[bank_u][:, 0:512], sg, ALU.mult,
                   waits=((mm, mm_u), (ac, ac.n)), inc=dv)
            sg_guards[j % 2] = dv.n
            bld.guard_bank(bank_u, dv, dv.n)
        h_done = (dv, dv.n)
        mm_gu_all = mm.n
        if stop_after == "gateup":
            ar.reset(mark0)
            done = h_done
            continue
        ar.reset(mark1)
        ytile = [ar.alloc(128, 512, BF16) for _ in range(FC)]
        y2t = [ar.alloc(128, 512, BF16) for _ in range(2)]
        wd = [ar.alloc(128, IC_FFN * 128, BF16) for _ in range(2)]
        slot_d = [bld.dma_slot(f"l0d{i}_h{h}") for i in range(2)]
        mm_after_mc = {}
        for oc in range(FC):
            buf = wd[oc % 2]
            dep = max(mm_after_mc.get(oc - 2, 0), mm_gu_all)
            qv = slot_d[oc % 2].fill(buf, ap3(deq_dn, oc * 128 * IC_FFN * 128,
                                              [[IC_FFN * 128, 128], [1, IC_FFN * 128]]),
                                     waits=((mm, dep), h_done))
            bank = 4 + (oc % 2)
            for kc in range(IC_FFN):
                w0 = bld.use_bank(bank, (qv, h_done)) if kc == 0 else ()
                bld.mm(bld.psum[bank][:, 0:512], buf[:, kc * 128:(kc + 1) * 128], htile[kc],
                       kc == 0, kc == IC_FFN - 1, waits=w0,
                       inc=mm if kc == IC_FFN - 1 else None)
            mm_y = mm.n
            mm_after_mc[oc] = mm_y
            y2 = y2t[oc % 2]
            bld.cp("vector", ytile[oc], bld.psum[bank][:, 0:512], waits=((mm, mm_y),), inc=dv)
            bld.guard_bank(bank, dv, dv.n)
            bld.act(y2, ytile[oc], AF.Square, waits=((dv, dv.n),), inc=ac)
            if stop_after != "down":
                w0 = bld.use_bank(6, ((ac, ac.n),)) if oc == 0 else ((ac, ac.n),)
                bld.mm(bld.psum[6][0:1, 0:512], cm.ones_col_bf[:], y2, oc == 0, oc == FC - 1,
                       waits=w0, inc=mm if oc == FC - 1 else None)
        if stop_after == "down":
            for oc in range(FC):
                bld.cp("vector", x_tiles[oc][:, sl], ytile[oc])
            done = (dv, dv.n)
            ar.reset(mark0)
            continue
        mm_stats = mm.n
        rows = ar.alloc(1, 512, F32)
        eps_ap, eps_rdy = cm.constf(1e-6)
        lhalf_ap, lhalf_rdy = cm.constf(float(math.log(0.5)))
        bld.act(rows, bld.psum[6][0:1, 0:512], AF.Ln, bias=eps_ap[0:1, :], scale=1.0 / D,
                waits=((mm, mm_stats), eps_rdy, lhalf_rdy), inc=ac)
        bld.act(rows, rows, AF.Exp, scale=-0.5, bias=lhalf_ap[0:1, :], inc=ac)
        ac_r = ac.n
        bld.guard_bank(6, ac, ac_r)
        w0 = bld.use_bank(6, ((ac, ac_r),))
        bld.mm(bld.psum[6][:, 0:512], cm.ones_row_f[:], rows, True, True, waits=w0, inc=mm)
        mm_bc = mm.n
        tmp = ar.alloc(128, 512, F32)
        for oc in range(FC):
            bld.tt("vector", tmp, ytile[oc], bld.psum[6][:, 0:512], ALU.mult,
                   waits=((mm, mm_bc),), inc=dv)
            bld.tt("vector", x_tiles[oc][:, sl], x_tiles[oc][:, sl], tmp, ALU.add, inc=dv)
        bld.guard_bank(6, dv, dv.n)
        done = (dv, dv.n)
        ar.reset(mark0)
    return done



def _declare_consts(nc):
    hb = {}
    hb["c_ones_col"] = nc.dram_tensor("c_ones_col", [128, 1], BF16, kind="ExternalInput").ap()
    hb["c_ones_row"] = nc.dram_tensor("c_ones_row", [1, 128], BF16, kind="ExternalInput").ap()
    hb["c_ones_col_f"] = nc.dram_tensor("c_ones_col_f", [128, 1], F32, kind="ExternalInput").ap()
    hb["c_ones_row_f"] = nc.dram_tensor("c_ones_row_f", [1, 128], F32, kind="ExternalInput").ap()
    hb["c_ident"] = nc.dram_tensor("c_ident", [128, 128], BF16, kind="ExternalInput").ap()
    hb["c_ints"] = nc.dram_tensor("c_ints", [128, 4], I32, kind="ExternalInput").ap()
    hb["c_ident_f"] = nc.dram_tensor("c_ident_f", [128, 128], F32, kind="ExternalInput").ap()
    return hb


def const_arrays():
    return {
        "c_ones_col": np.ones((128, 1), BF),
        "c_ones_row": np.ones((1, 128), BF),
        "c_ones_col_f": np.ones((128, 1), np.float32),
        "c_ones_row_f": np.ones((1, 128), np.float32),
        "c_ident": np.eye(128, dtype=np.float32).astype(BF),
        "c_ident_f": np.eye(128, dtype=np.float32),
        "c_ints": np.stack([4 * (np.arange(128, dtype=np.int32) % 8),
                            np.full(128, 8, np.int32), np.full(128, 7, np.int32),
                            np.full(128, 11, np.int32)], axis=1),
    }


def build_deq_program():
    nc = bass.Bass("TRN2", target_bir_lowering=False)
    hb = _declare_consts(nc)
    hb["p_gu"] = nc.dram_tensor("p_gu", [256, 1408], I32, kind="ExternalInput").ap()
    hb["s_gu"] = nc.dram_tensor("s_gu", [2048, 1408], F16, kind="ExternalInput").ap()
    hb["p_dn"] = nc.dram_tensor("p_dn", [704, 256], I32, kind="ExternalInput").ap()
    hb["s_dn"] = nc.dram_tensor("s_dn", [5632, 256], F16, kind="ExternalInput").ap()
    hb["deq_gu_loc"] = nc.dram_tensor("deq_gu_loc", [DEQ_GU_PER_CORE, 128, 16, 128], BF16,
                                      kind="ExternalOutput").ap()
    hb["deq_dn_loc"] = nc.dram_tensor("deq_dn_loc", [DEQ_DN_PER_CORE, 128, 44, 128], BF16,
                                      kind="ExternalOutput").ap()
    bld = B(nc)
    with bld.es:
        bld.alloc_psum_banks()
        cm = Common(bld, hb)
        ar = Arena(bld, "main", 100 * 1024)
        S = {k: bld.sem(k) for k in ("mm", "dv", "ac", "gp", "qg", "qs", "qw")}
        emit_dequant(bld, cm, ar, S, hb)
        bld.join_all()
        bld.run_block()
    return nc


def build_l0ffn_program():
    nc = bass.Bass("TRN2", target_bir_lowering=False)
    hb = _declare_consts(nc)
    hb["x_in"] = nc.dram_tensor("x_in", [D, TLOC], F32, kind="ExternalInput").ap()
    hb["deq_gu"] = nc.dram_tensor("deq_gu", [88, 128, 16, 128], BF16, kind="ExternalInput").ap()
    hb["deq_dn"] = nc.dram_tensor("deq_dn", [16, 128, 44, 128], BF16, kind="ExternalInput").ap()
    hb["x_out"] = nc.dram_tensor("x_out", [D, TLOC], F32, kind="ExternalOutput").ap()
    bld = B(nc)
    with bld.es:
        bld.alloc_psum_banks()
        cm = Common(bld, hb)
        ar = Arena(bld, "main", 130 * 1024)
        S = {k: bld.sem(k) for k in ("mm", "dv", "ac", "gp", "qg", "qs", "qw", "qx")}
        x_tiles = [bld.nc_sbuf(f"x{c}", [128, TLOC], F32) for c in range(FC)]
        qx = S["qx"]
        for c in range(FC):
            bld.emit("sync", lambda e, c=c: e.dma_start(
                out=x_tiles[c][:], in_=hb["x_in"][c * 128:(c + 1) * 128, :]),
                inc=qx, amt=16)
        x_ready = (qx, qx.n)
        done = emit_l0ffn(bld, cm, ar, S, hb, x_tiles, x_ready)
        for c in range(FC):
            bld.emit("sync", lambda e, c=c: e.dma_start(
                out=hb["x_out"][c * 128:(c + 1) * 128, :], in_=x_tiles[c][:]),
                waits=(done,), inc=S["qs"], amt=16)
        bld.join_all()
        bld.run_block()
    return nc



def host_prep_l0ffn(inputs):
    gu_p = np.asarray(inputs["L0_ffn_gate_up_packed"])
    gu_s = np.asarray(inputs["L0_ffn_gate_up_scales"])
    dn_p = np.asarray(inputs["L0_ffn_down_packed"])
    dn_s = np.asarray(inputs["L0_ffn_down_scales"])
    guT = np.ascontiguousarray(gu_p.T)
    dnT = np.ascontiguousarray(dn_p.T)
    gu_srep = np.repeat(gu_s, 128, axis=0)
    dn_srep = np.repeat(dn_s, 128, axis=0)
    per = []
    for c in range(NCORES):
        g0 = c * DEQ_GU_PER_CORE * 128
        g1 = g0 + DEQ_GU_PER_CORE * 128
        d0 = c * DEQ_DN_PER_CORE * 128
        d1 = d0 + DEQ_DN_PER_CORE * 128
        per.append({
            "p_gu": np.ascontiguousarray(guT[:, g0:g1]),
            "s_gu": np.ascontiguousarray(gu_srep[:, g0:g1]),
            "p_dn": np.ascontiguousarray(dnT[:, d0:d1]),
            "s_dn": np.ascontiguousarray(dn_srep[:, d0:d1]),
        })
    return per


def split_x(xfull_T):
    out = []
    for c in range(NCORES):
        a, b = strips_of(c)
        out.append(np.ascontiguousarray(
            np.concatenate([xfull_T[:, a * SW:(a + 1) * SW],
                            xfull_T[:, b * SW:(b + 1) * SW]], axis=1)))
    return out


def merge_x(shards):
    full = np.zeros((D, T), np.float32)
    for c in range(NCORES):
        a, b = strips_of(c)
        full[:, a * SW:(a + 1) * SW] = shards[c][:, 0:SW]
        full[:, b * SW:(b + 1) * SW] = shards[c][:, SW:2 * SW]
    return full


def kernel(**inputs):
    raise NotImplementedError("under construction")



def emit_l1ffn(bld, cm, ar, S, hb, x_tiles, x_ready):
    mm, dv, ac = S["mm"], S["dv"], S["ac"]
    q1, q2, q3 = hb["q_w1"], hb["q_w2"], hb["q_w3"]
    done = x_ready
    mark0 = ar.mark()
    for h in range(2):
        sl = slice(h * 512, (h + 1) * 512)
        htile = [ar.alloc(128, 512, BF16) for _ in range(IC_FFN)]
        mark1 = ar.mark()
        xn = [ar.alloc(128, 512, BF16) for _ in range(FC)]
        xs = [x_tiles[c][:, sl] for c in range(FC)]
        ln_done = ln_fmajor(bld, cm, ar, xs, xn, 512, S, x_ready=(done,))
        sgt = [ar.alloc(128, 512, BF16) for _ in range(2)]
        wg = [ar.alloc(128, 2048, BF16) for _ in range(2)]
        wu = [ar.alloc(128, 2048, BF16) for _ in range(2)]
        slot_g = [bld.dma_slot(f"l1g{i}_h{h}") for i in range(2)]
        slot_u = [bld.dma_slot(f"l1u{i}_h{h}") for i in range(2)]
        mm_after_pair = {}
        sg_guards = {}
        for j in range(IC_FFN):
            bg, bu = wg[j % 2], wu[j % 2]
            dep = mm_after_pair.get(j - 2, 0)
            qv_g = slot_g[j % 2].fill(bg, ap3(q1, j * 128 * 2048, [[2048, 128], [1, 2048]]),
                                      waits=((mm, dep),))
            qv_u = slot_u[j % 2].fill(bu, ap3(q2, j * 128 * 2048, [[2048, 128], [1, 2048]]))
            bank_g, bank_u = (2 * j) % 4, (2 * j + 1) % 4
            for kc in range(FC):
                w0 = bld.use_bank(bank_g, (qv_g, ln_done)) if kc == 0 else ()
                bld.mm(bld.psum[bank_g][:, 0:512], bg[:, kc * 128:(kc + 1) * 128], xn[kc],
                       kc == 0, kc == FC - 1, waits=w0, inc=mm if kc == FC - 1 else None)
            mm_g = mm.n
            for kc in range(FC):
                w0 = bld.use_bank(bank_u, (qv_u,)) if kc == 0 else ()
                bld.mm(bld.psum[bank_u][:, 0:512], bu[:, kc * 128:(kc + 1) * 128], xn[kc],
                       kc == 0, kc == FC - 1, waits=w0, inc=mm if kc == FC - 1 else None)
            mm_u = mm.n
            mm_after_pair[j] = mm.n
            sg = sgt[j % 2]
            sg_guard = sg_guards.get(j % 2, 0)
            bld.act(sg, bld.psum[bank_g][:, 0:512], AF.Sigmoid,
                    waits=((mm, mm_g), (dv, sg_guard)), inc=ac)
            bld.guard_bank(bank_g, ac, ac.n)
            bld.tt("vector", htile[j], bld.psum[bank_u][:, 0:512], sg, ALU.mult,
                   waits=((mm, mm_u), (ac, ac.n)), inc=dv)
            sg_guards[j % 2] = dv.n
            bld.guard_bank(bank_u, dv, dv.n)
        h_done = (dv, dv.n)
        mm_gu_all = mm.n
        ar.reset(mark1)
        wd = [ar.alloc(128, IC_FFN * 128, BF16) for _ in range(2)]
        slot_d = [bld.dma_slot(f"l1d{i}_h{h}") for i in range(2)]
        mm_after_mc = {}
        for oc in range(FC):
            buf = wd[oc % 2]
            dep = max(mm_after_mc.get(oc - 2, 0), mm_gu_all)
            qv = slot_d[oc % 2].fill(buf, ap3(q3, oc * 128 * IC_FFN * 128,
                                              [[IC_FFN * 128, 128], [1, IC_FFN * 128]]),
                                     waits=((mm, dep),))
            bank = 4 + (oc % 2)
            for kc in range(IC_FFN):
                w0 = bld.use_bank(bank, (qv, h_done)) if kc == 0 else ()
                bld.mm(bld.psum[bank][:, 0:512], buf[:, kc * 128:(kc + 1) * 128], htile[kc],
                       kc == 0, kc == IC_FFN - 1, waits=w0,
                       inc=mm if kc == IC_FFN - 1 else None)
            mm_y = mm.n
            mm_after_mc[oc] = mm_y
            bld.stt("vector", x_tiles[oc][:, sl], bld.psum[bank][:, 0:512], 0.5,
                    x_tiles[oc][:, sl], ALU.mult, ALU.add, waits=((mm, mm_y),), inc=dv)
            bld.guard_bank(bank, dv, dv.n)
        done = (dv, dv.n)
        ar.reset(mark0)
    return done



def emit_l3ffn(bld, cm, ar, S, hb, x_tiles, x_ready, tag=""):
    mm, dv, ac = S["mm"], S["dv"], S["ac"]
    done = x_ready
    mark0 = ar.mark()
    for h in range(2):
        sl = slice(h * 512, (h + 1) * 512)
        mid = [ar.alloc(128, 512, BF16) for _ in range(FC)]
        xn = [ar.alloc(128, 512, BF16) for _ in range(FC)]
        xs = [x_tiles[c][:, sl] for c in range(FC)]
        ln_done = ln_fmajor(bld, cm, ar, xs, xn, 512, S, x_ready=(done,))
        wu = [ar.alloc(128, 2048, BF16) for _ in range(2)]
        slot_u = [bld.dma_slot(f"l3u{i}_h{h}{tag}", "gpsimd") for i in range(2)]
        mm_after = {}
        for mc in range(FC):
            buf = wu[mc % 2]
            dep = mm_after.get(mc - 2, 0)
            qv = slot_u[mc % 2].fill(buf, ap3(hb["l3_up"], mc * 128 * 2048,
                                              [[2048, 128], [1, 2048]]),
                                     waits=((mm, dep),))
            bank = mc % 2
            for kc in range(FC):
                w0 = bld.use_bank(bank, (qv, ln_done)) if kc == 0 else ()
                bld.mm(bld.psum[bank][:, 0:512], buf[:, kc * 128:(kc + 1) * 128], xn[kc],
                       kc == 0, kc == FC - 1, waits=w0, inc=mm if kc == FC - 1 else None)
            mm_after[mc] = mm.n
            bld.act(mid[mc], bld.psum[bank][:, 0:512], AF.Relu,
                    waits=((mm, mm_after[mc]),), inc=ac)
            bld.guard_bank(bank, ac, ac.n)
        mid_done = (ac, ac.n)
        mm_up_all = mm.n
        wd = [ar.alloc(128, 2048, BF16) for _ in range(2)]
        we = [ar.alloc(128, 2048, BF16) for _ in range(2)]
        slot_d = [bld.dma_slot(f"l3d{i}_h{h}{tag}", "gpsimd") for i in range(2)]
        slot_e = [bld.dma_slot(f"l3e{i}_h{h}{tag}", "gpsimd") for i in range(2)]
        mm_after2 = {}
        for oc in range(FC):
            bufd, bufe = wd[oc % 2], we[oc % 2]
            dep = max(mm_after2.get(oc - 2, 0), mm_up_all)
            qv_d = slot_d[oc % 2].fill(bufd, ap3(hb["l3_dn"], oc * 128 * 2048,
                                                 [[2048, 128], [1, 2048]]),
                                       waits=((mm, dep),))
            qv_e = slot_e[oc % 2].fill(bufe, ap3(hb["l3_ex"], oc * 128 * 2048,
                                                 [[2048, 128], [1, 2048]]))
            bank = 2 + (oc % 2)
            for kc in range(FC):
                w0 = bld.use_bank(bank, (qv_d, qv_e, mid_done)) if kc == 0 else ()
                bld.mm(bld.psum[bank][:, 0:512], bufd[:, kc * 128:(kc + 1) * 128], mid[kc],
                       kc == 0, False, waits=w0)
            for kc in range(FC):
                bld.mm(bld.psum[bank][:, 0:512], bufe[:, kc * 128:(kc + 1) * 128], xn[kc],
                       False, kc == FC - 1, inc=mm if kc == FC - 1 else None)
            mm_after2[oc] = mm.n
            bld.stt("vector", x_tiles[oc][:, sl], bld.psum[bank][:, 0:512], 0.5,
                    x_tiles[oc][:, sl], ALU.mult, ALU.add, waits=((mm, mm_after2[oc]),), inc=dv)
            bld.guard_bank(bank, dv, dv.n)
        done = (dv, dv.n)
        ar.reset(mark0)
    return done



def emit_l2ffn(bld, cm, ar, S, hb, x_tiles, x_ready, tag=""):
    mm, dv, ac = S["mm"], S["dv"], S["ac"]
    done = x_ready
    mark0 = ar.mark()
    NE = 18
    DMID = 1024
    MCU = DMID // 128
    for h in range(2):
        sl = slice(h * 512, (h + 1) * 512)
        xn = [ar.alloc(128, 512, BF16) for _ in range(FC)]
        out_acc = [ar.alloc(128, 512, F32) for _ in range(FC)]
        gateT = ar.alloc(16, 512, F32)
        mark1 = ar.mark()
        xs = [x_tiles[c][:, sl] for c in range(FC)]
        ln_done = ln_fmajor(bld, cm, ar, xs, xn, 512, S, x_ready=(done,))
        wrt = ar.alloc(128, 16 * 16, BF16)
        slot_r = bld.dma_slot(f"l2r_h{h}{tag}", "gpsimd")
        qv_r = slot_r.fill(wrt, ap3(hb["l2_rt"], 0, [[16, 128], [2048, 16], [1, 16]]))
        probs = ar.alloc(128, 4 * 16, F32)
        rowsc = ar.alloc(128, 24, F32)
        for tc in range(4):
            bank = 5
            w0 = bld.use_bank(bank, (qv_r, ln_done)) if tc == 0 else ()
            for kc in range(FC):
                bld.mm(bld.psum[bank][0:128, 0:16], xn[kc][:, tc * 128:(tc + 1) * 128],
                       wrt[:, kc * 16:(kc + 1) * 16], kc == 0, kc == FC - 1,
                       waits=w0 if kc == 0 else (),
                       inc=mm if kc == FC - 1 else None)
            mmv = mm.n
            pr = probs[:, tc * 16:(tc + 1) * 16]
            m1 = rowsc[:, 0:1]
            m2 = rowsc[:, 1:2]
            sm = rowsc[:, 2:3]
            bld.red(m1, bld.psum[bank][0:128, 0:16], ALU.max, waits=((mm, mmv),), inc=dv)
            bld.guard_bank(bank, dv, dv.n)
            bld.ts("vector", pr, bld.psum[bank][0:128, 0:16], m1, -1.0, ALU.subtract, ALU.mult)
            bld.guard_bank(bank, dv, dv.n)
            acw = (dv, dv.n)
            bld.act(pr, pr, AF.Exp, scale=-1.0, waits=(acw,), inc=ac)
            bld.red(sm, pr, ALU.add, waits=((ac, ac.n),), inc=dv)
            bld.recip(sm, sm)
            bld.ts("vector", pr, pr, sm, 0.0, ALU.mult, ALU.add)
            bld.red(m1, pr, ALU.max)
            bld.ts("vector", rowsc[:, 4:20], pr, m1, -1e30, ALU.is_ge, ALU.mult)
            bld.tt("vector", rowsc[:, 4:20], rowsc[:, 4:20], pr, ALU.add)
            bld.red(m2, rowsc[:, 4:20], ALU.max)
            bld.ts("vector", rowsc[:, 4:20], pr, m2, 1.0, ALU.is_ge, ALU.mult)
            bld.tt("vector", pr, pr, rowsc[:, 4:20], ALU.mult)
            prdone = (dv, dv.n)
            bank2 = 6
            w0 = bld.use_bank(bank2, (prdone, cm.ready))
            bld.transp(bld.psum[bank2][0:16, 0:128], pr, cm.ident_f32[:], waits=w0,
                       inc=mm)
            bld.cp("vector", gateT[:, tc * 128:(tc + 1) * 128], bld.psum[bank2][0:16, 0:128],
                   waits=((mm, mm.n),), inc=dv)
            bld.guard_bank(bank2, dv, dv.n)
        gate_done = (dv, dv.n)
        gate_row = [ar.alloc(1, 512, F32) for _ in range(2)]
        gate_row_guard = {}
        up_bufs = [ar.alloc(128, 2048, BF16) for _ in range(2)]
        dn_bufs = [ar.alloc(128, DMID, BF16) for _ in range(2)]
        slot_up = [bld.dma_slot(f"l2u{i}_h{h}{tag}", "gpsimd") for i in range(2)]
        slot_dn = [bld.dma_slot(f"l2d{i}_h{h}{tag}", "gpsimd") for i in range(2)]
        mid = [ar.alloc(128, 512, BF16) for _ in range(MCU)]
        mm_up_prev = {}
        mm_dn_prev = {}
        up_sz = MCU * 128 * FC * 128
        dn_sz = FC * 128 * MCU * 128
        for e in range(NE):
            routed = e >= 2
            if routed:
                gb = bld.use_bank(7, (gate_done, cm.ready))
                bld.mm(bld.psum[7][0:1, 0:512], cm.ident_f32[0:16, e - 2:e - 1],
                       gateT[0:16, 0:512], True, True, waits=gb, inc=mm)
                grow_sb = gate_row[e % 2]
                bld.cp("vector", grow_sb, bld.psum[7][0:1, 0:512],
                       waits=((mm, mm.n), (dv, gate_row_guard.get(e % 2, 0))), inc=dv)
                bld.guard_bank(7, dv, dv.n)
                gb2 = bld.use_bank(7, ())
                bld.mm(bld.psum[7][:, 0:512], cm.ones_row_f[:], grow_sb, True, True,
                       waits=gb2, inc=mm)
                mm_gate = mm.n
            for mc in range(MCU):
                buf = up_bufs[mc % 2]
                dep = mm_up_prev.get((e * MCU + mc) - 2, 0)
                qv = slot_up[mc % 2].fill(
                    buf, ap3(hb["l2_up"], e * up_sz + mc * 128 * 2048,
                             [[2048, 128], [1, 2048]]), waits=((mm, dep),))
                bank = mc % 2
                for kc in range(FC):
                    w0 = bld.use_bank(bank, (qv, ln_done)) if kc == 0 else ()
                    bld.mm(bld.psum[bank][:, 0:512], buf[:, kc * 128:(kc + 1) * 128], xn[kc],
                           kc == 0, kc == FC - 1, waits=w0, inc=mm if kc == FC - 1 else None)
                mm_up_prev[e * MCU + mc] = mm.n
                bld.act(mid[mc], bld.psum[bank][:, 0:512], AF.Gelu,
                        waits=((mm, mm.n),), inc=ac)
                bld.guard_bank(bank, ac, ac.n)
                if routed:
                    bld.tt("vector", mid[mc], mid[mc], bld.psum[7][:, 0:512], ALU.mult,
                           waits=((ac, ac.n), (mm, mm_gate)), inc=dv)
            mid_done = (dv, dv.n) if routed else (ac, ac.n)
            if routed:
                bld.guard_bank(7, dv, dv.n)
                gate_row_guard[e % 2] = dv.n
            for oc in range(FC):
                buf = dn_bufs[oc % 2]
                dep = mm_dn_prev.get((e * FC + oc) - 2, 0)
                qv = slot_dn[oc % 2].fill(
                    buf, ap3(hb["l2_dn"], e * dn_sz + oc * 128 * DMID,
                             [[DMID, 128], [1, DMID]]), waits=((mm, dep),))
                bank = 2 + (oc % 2)
                for kc in range(MCU):
                    w0 = bld.use_bank(bank, (qv, mid_done)) if kc == 0 else ()
                    bld.mm(bld.psum[bank][:, 0:512], buf[:, kc * 128:(kc + 1) * 128], mid[kc],
                           kc == 0, kc == MCU - 1, waits=w0, inc=mm if kc == MCU - 1 else None)
                mm_dn_prev[e * FC + oc] = mm.n
                if e == 0:
                    bld.cp("vector", out_acc[oc], bld.psum[bank][:, 0:512],
                           waits=((mm, mm.n),), inc=dv)
                else:
                    bld.tt("vector", out_acc[oc], out_acc[oc], bld.psum[bank][:, 0:512],
                           ALU.add, waits=((mm, mm.n),), inc=dv)
                bld.guard_bank(bank, dv, dv.n)
        for oc in range(FC):
            bld.stt("vector", x_tiles[oc][:, sl], out_acc[oc], 0.5, x_tiles[oc][:, sl],
                    ALU.mult, ALU.add, inc=dv)
        done = (dv, dv.n)
        ar.reset(mark0)
    return done



NQ_ELEMS = 3 * I_FFN * D
NQ_COLS_PER_CORE = NQ_ELEMS // NCORES // 128
NQ_B1 = I_FFN * D // 128
NQ_B2 = 2 * I_FFN * D // 128
NQ_N = float(I_FFN * D)


NQ_SEG = NQ_COLS_PER_CORE // 3


def _mat_ranges(core=None):
    return [(m, m * NQ_SEG, (m + 1) * NQ_SEG) for m in range(3)]


def emit_l1prep_a(bld, cm, ar, S, hb):
    dv, mm = S["dv"], S["mm"]
    acc = ar.alloc(128, 3, F32)
    red = ar.alloc(128, 1, F32)
    bld.mset("vector", acc, 0.0)
    CW = 2048
    tiles = [ar.alloc(128, CW, F32) for _ in range(2)]
    slots = [bld.dma_slot(f"qa{i}", "sync") for i in range(2)]
    nch = NQ_COLS_PER_CORE // CW + (1 if NQ_COLS_PER_CORE % CW else 0)
    ranges = _mat_ranges()
    consumed = {}
    for ch in range(nch):
        c0 = ch * CW
        w = min(CW, NQ_COLS_PER_CORE - c0)
        t = tiles[ch % 2]
        dep = consumed.get(ch - 2, 0)
        qv = slots[ch % 2].fill(t[:, 0:w], ap3(hb["qsrc"], c0,
                                               [[NQ_COLS_PER_CORE, 128], [1, w]]),
                                waits=((dv, dep),))
        for m, lo, hi in ranges:
            a, b = max(lo, c0), min(hi, c0 + w)
            if a >= b:
                continue
            bld.red(red, t[:, a - c0:b - c0], ALU.add, absval=True, waits=(qv,))
            bld.tt("vector", acc[:, m:m + 1], acc[:, m:m + 1], red, ALU.add)
        consumed[ch] = dv.n
    w0 = bld.use_bank(0, ((dv, dv.n), cm.ready))
    bld.mm(bld.psum[0][0:1, 0:3], cm.ones_col_f[:], acc, True, True, waits=w0, inc=mm)
    bld.cp("vector", ar.alloc(1, 3, F32), bld.psum[0][0:1, 0:3], waits=((mm, mm.n),))
    gp_out = ar.alloc(1, 3, F32)
    bld.cp("vector", gp_out, bld.psum[0][0:1, 0:3])
    bld.guard_bank(0, dv, dv.n)
    bld.dma("sync", hb["g_part"][0:1, :], gp_out, waits=((dv, dv.n),), inc=S["qs"])


def emit_l1prep_b(bld, cm, ar, S, hb):
    dv, mm, ac = S["dv"], S["mm"], S["ac"]
    gall = ar.alloc(8, 3, F32)
    slot0 = bld.dma_slot("qb0", "sync")
    qv0 = slot0.fill(gall, hb["g_all"][:, :])
    grow = ar.alloc(1, 3, F32)
    w0 = bld.use_bank(0, (qv0, cm.ready))
    bld.mm(bld.psum[0][0:1, 0:3], cm.ones_col_f[0:8, :], gall, True, True, waits=w0, inc=mm)
    bld.ts("vector", grow, bld.psum[0][0:1, 0:3], 1.0 / NQ_N, 1e-8, ALU.mult, ALU.add,
           waits=((mm, mm.n),))
    bld.guard_bank(0, dv, dv.n)
    bld.recip(grow, grow)
    dvg = dv.n
    rec_cols = ar.alloc(128, 3, F32)
    grow_bf = ar.alloc(1, 3, F32)
    bld.cp("vector", grow_bf, grow)
    for m in range(3):
        w0 = bld.use_bank(1 + m, ((dv, dv.n),))
        bld.mm(bld.psum[1 + m][:, 0:1], cm.ones_row_f[:], grow[0:1, m:m + 1], True, True,
               waits=w0, inc=mm)
        bld.cp("vector", rec_cols[:, m:m + 1], bld.psum[1 + m][:, 0:1], waits=((mm, mm.n),))
        bld.guard_bank(1 + m, dv, dv.n)
    rc_done = dv.n
    CW = 2048
    tiles = [ar.alloc(128, CW, F32) for _ in range(2)]
    qtiles = [ar.alloc(128, CW, BF16) for _ in range(2)]
    it = ar.alloc(128, CW, I32)
    slots = [bld.dma_slot(f"qb{i+1}", "sync") for i in range(2)]
    ost = [bld.dma_slot(f"qbo{i}", "sync") for i in range(2)]
    st_done = {}
    nch = NQ_COLS_PER_CORE // CW + (1 if NQ_COLS_PER_CORE % CW else 0)
    ranges = _mat_ranges()
    consumed = {}
    for ch in range(nch):
        c0 = ch * CW
        w = min(CW, NQ_COLS_PER_CORE - c0)
        t = tiles[ch % 2]
        q = qtiles[ch % 2]
        dep = consumed.get(ch - 2, 0)
        qv = slots[ch % 2].fill(t[:, 0:w], ap3(hb["qsrc"], c0,
                                               [[NQ_COLS_PER_CORE, 128], [1, w]]),
                                waits=((dv, dep),))
        sdep = st_done.get(ch - 2, None)
        for m, lo, hi in ranges:
            a, b = max(lo, c0), min(hi, c0 + w)
            if a >= b:
                continue
            ww = [qv, (dv, rc_done)] + ([sdep] if sdep else [])
            bld.ts("vector", t[:, a - c0:b - c0], t[:, a - c0:b - c0],
                   rec_cols[:, m:m + 1], 0.0, ALU.mult, ALU.add, waits=tuple(ww))
        bld.ts("vector", it.bitcast(F32)[:, 0:w], t[:, 0:w], 0.5, 1.0, ALU.is_ge, ALU.mult)
        bld.ts("vector", t[:, 0:w], t[:, 0:w], -0.5, 1.0, ALU.is_le, ALU.mult)
        bld.tt("vector", q[:, 0:w], it.bitcast(F32)[:, 0:w], t[:, 0:w], ALU.subtract)
        consumed[ch] = dv.n
        st_done[ch] = ost[ch % 2].fill(
            ap3(hb["q_loc"], c0, [[NQ_COLS_PER_CORE, 128], [1, w]]), q[:, 0:w],
            waits=((dv, dv.n),))


def host_prep_l1ffn(inputs):
    w1 = np.asarray(inputs["L1_ffn_w1"])
    w2 = np.asarray(inputs["L1_ffn_w2"])
    w3 = np.asarray(inputs["L1_ffn_w3"])
    t1 = tile_lhsT(np.ascontiguousarray(w1.T))
    t2 = tile_lhsT(np.ascontiguousarray(w2.T))
    t3 = tile_lhsT(np.ascontiguousarray(w3.T))
    flats = [t1.ravel(), t2.ravel(), t3.ravel()]
    per = []
    n = NQ_SEG * 128
    for c in range(NCORES):
        sl = np.concatenate([f[c * n:(c + 1) * n] for f in flats])
        per.append({"qsrc": np.ascontiguousarray(sl.reshape(NQ_COLS_PER_CORE, 128).T)})
    return per


def host_assemble_l1q(q_locs):
    n = NQ_SEG * 128
    flats = [np.ascontiguousarray(q.T).ravel() for q in q_locs]
    segs = [np.concatenate([f[m * n:(m + 1) * n] for f in flats])
            for m in range(3)]
    q1 = segs[0].reshape(44, 128, 16, 128)
    q2 = segs[1].reshape(44, 128, 16, 128)
    q3 = segs[2].reshape(16, 128, 44, 128)
    return q1, q2, q3



def build_stage_program(name, extra_ins, emitter, arena_kb=132, with_x=True,
                        extra_outs=()):
    nc = bass.Bass("TRN2", target_bir_lowering=False)
    hb = _declare_consts(nc)
    if with_x:
        hb["x_in"] = nc.dram_tensor("x_in", [D, TLOC], F32, kind="ExternalInput").ap()
        hb["x_out"] = nc.dram_tensor("x_out", [D, TLOC], F32, kind="ExternalOutput").ap()
    for nm, shape, dt in extra_ins:
        hb[nm] = nc.dram_tensor(nm, list(shape), dt, kind="ExternalInput").ap()
    for nm, shape, dt in extra_outs:
        hb[nm] = nc.dram_tensor(nm, list(shape), dt, kind="ExternalOutput").ap()
    bld = B(nc)
    with bld.es:
        bld.alloc_psum_banks()
        cm = Common(bld, hb)
        ar = Arena(bld, "main", arena_kb * 1024)
        S = {k: bld.sem(k) for k in ("mm", "dv", "ac", "gp", "qg", "qs", "qw", "qx")}
        if with_x:
            x_tiles = [bld.nc_sbuf(f"x{c}", [128, TLOC], F32) for c in range(FC)]
            qx = S["qx"]
            for c in range(FC):
                bld.dma("sync", x_tiles[c][:], hb["x_in"][c * 128:(c + 1) * 128, :], inc=qx)
            x_ready = (qx, qx.n)
        else:
            x_tiles, x_ready = None, None
        done = emitter(bld, cm, ar, S, hb, x_tiles, x_ready)
        if with_x:
            for c in range(FC):
                bld.dma("sync", hb["x_out"][c * 128:(c + 1) * 128, :], x_tiles[c][:],
                        waits=(done,), inc=S["qs"])
        bld.join_all()
        bld.run_block()
    return nc


def host_prep_l3ffn(inputs):
    up = tile_lhsT(np.ascontiguousarray(np.asarray(inputs["L3_ffn_W_up"]).T))
    dn = tile_lhsT(np.ascontiguousarray(np.asarray(inputs["L3_ffn_W_down"]).T))
    ex = tile_lhsT(np.ascontiguousarray(np.asarray(inputs["L3_ffn_W_expert"])))
    return {"l3_up": up, "l3_dn": dn, "l3_ex": ex}


def host_prep_l2ffn(inputs):
    shu = np.asarray(inputs["L2_ffn_Wsh_up"])
    shd = np.asarray(inputs["L2_ffn_Wsh_down"])
    rtu = np.asarray(inputs["L2_ffn_Wrt_up"])
    rtd = np.asarray(inputs["L2_ffn_Wrt_down"])
    ups = np.stack([tile_lhsT(shu[i]) for i in range(2)] +
                   [tile_lhsT(rtu[i]) for i in range(16)])
    dns = np.stack([tile_lhsT(shd[i]) for i in range(2)] +
                   [tile_lhsT(rtd[i]) for i in range(16)])
    rt = np.ascontiguousarray(np.asarray(inputs["L2_ffn_W_router"]))
    return {"l2_up": ups, "l2_dn": dns, "l2_rt": rt}



def emit_l3attn(bld, cm, ar, S, hb, x_tiles, x_ready):
    mm, dv, ac = S["mm"], S["dv"], S["ac"]
    NB, H = 4, 16
    mark_outer = ar.mark()
    mem_bf = ar.alloc(128, NB * H * 128, BF16)
    mn_hn = ar.alloc(128, 64, BF16)
    mn_nh = ar.alloc(128, 64, F32)
    bw = ar.alloc(16, 4, F32)
    slot_m = bld.dma_slot("l3am", "gpsimd")
    qm = slot_m.fill(mem_bf, ap3(hb["l3a_mem"], 0, [[128, 128], [16384, NB * H], [1, 128]]))
    slot_m2 = bld.dma_slot("l3am2", "gpsimd")
    qm2 = slot_m2.fill(mn_hn, hb["l3a_mn_hn"][:, :])
    slot_m4 = bld.dma_slot("l3am4", "sync")
    qm4 = slot_m4.fill(mn_nh, hb["l3a_mn_nh"][:, :])
    slot_m3 = bld.dma_slot("l3am3", "sync")
    qm3 = slot_m3.fill(bw, hb["l3a_bw"][:, :])
    m1 = ar.alloc(16, 1, F32)
    bld.red(m1, bw, ALU.max, waits=(qm3,))
    bld.ts("vector", bw, bw, m1, -1.0, ALU.subtract, ALU.mult)
    bld.act(bw, bw, AF.Exp, scale=-1.0, waits=((dv, dv.n),), inc=ac)
    bld.red(m1, bw, ALU.add, waits=((ac, ac.n),))
    bld.recip(m1, m1)
    bld.ts("vector", bw, bw, m1, 0.0, ALU.mult, ALU.add)
    red128 = ar.alloc(128, 1, F32)
    arow = ar.alloc(1, 4, F32)
    for n in range(NB):
        bld.red(red128, mn_nh[:, n * 16:(n + 1) * 16], ALU.add,
                waits=(qm4, (mm, mm.n)))
        w0 = bld.use_bank(5, ((dv, dv.n), cm.ready))
        bld.mm(bld.psum[5][0:1, n:n + 1], cm.ones_col_f[:], red128, True, True,
               waits=w0, inc=mm)
        bld.guard_bank(5, mm, mm.n)
    bld.ts("vector", arow, bld.psum[5][0:1, 0:4], 1e-6, 1.0, ALU.is_ge, ALU.mult,
           waits=((mm, mm.n),))
    bld.guard_bank(5, dv, dv.n)
    w0 = bld.use_bank(5, ((dv, dv.n),))
    bld.mm(bld.psum[5][0:16, 0:4], cm.ones_row_f[0:1, 0:16], arow, True, True,
           waits=w0, inc=mm)
    bld.tt("vector", bw, bw, bld.psum[5][0:16, 0:4], ALU.mult, waits=((mm, mm.n),))
    bld.guard_bank(5, dv, dv.n)
    wa_b = ar.alloc(128, 4 * H, F32)
    rowtmp = ar.alloc(1, 4, F32)
    for h in range(H):
        w0 = bld.use_bank(5, ((dv, dv.n), cm.ready))
        bld.mm(bld.psum[5][0:1, 0:4], cm.ident_f32[0:16, h:h + 1], bw, True, True,
               waits=w0, inc=mm)
        bld.cp("vector", rowtmp, bld.psum[5][0:1, 0:4], waits=((mm, mm.n),))
        bld.guard_bank(5, dv, dv.n)
        w0 = bld.use_bank(5, ((dv, dv.n),))
        bld.mm(bld.psum[5][:, 0:4], cm.ones_row_f[:], rowtmp, True, True, waits=w0, inc=mm)
        bld.cp("vector", wa_b[:, 4 * h:4 * h + 4], bld.psum[5][:, 0:4], waits=((mm, mm.n),))
        bld.guard_bank(5, dv, dv.n)
    prep_done = (dv, dv.n)

    done = x_ready
    for h2 in range(2):
        sl = slice(h2 * 512, (h2 + 1) * 512)
        mark = ar.mark()
        xn = [ar.alloc(128, 512, BF16) for _ in range(FC)]
        xs = [x_tiles[c][:, sl] for c in range(FC)]
        ln_done = ln_fmajor(bld, cm, ar, xs, xn, 512, S, x_ready=(done,))
        sq = [ar.alloc(128, 512, BF16) for _ in range(H)]
        elu_t = ar.alloc(128, 512, F32)
        wq = [ar.alloc(128, 2048, BF16) for _ in range(2)]
        slot_q = [bld.dma_slot(f"l3aq{i}_h{h2}", "gpsimd") for i in range(2)]
        mm_prev = {}
        for mc in range(H):
            buf = wq[mc % 2]
            dep = mm_prev.get(mc - 2, 0)
            qv = slot_q[mc % 2].fill(buf, ap3(hb["l3a_q"], mc * 128 * 2048,
                                              [[2048, 128], [1, 2048]]),
                                     waits=((mm, dep),))
            bank = mc % 2
            for kc in range(FC):
                w0 = bld.use_bank(bank, (qv, ln_done)) if kc == 0 else ()
                bld.mm(bld.psum[bank][:, 0:512], buf[:, kc * 128:(kc + 1) * 128], xn[kc],
                       kc == 0, kc == FC - 1, waits=w0, inc=mm if kc == FC - 1 else None)
            mm_prev[mc] = mm.n
            bld.ts("vector", elu_t, bld.psum[bank][:, 0:512], 0.0, 1.0, ALU.min, ALU.mult,
                   waits=((mm, mm.n),))
            bld.act(elu_t, elu_t, AF.Exp, waits=((dv, dv.n),), inc=ac)
            bld.stt("vector", sq[mc], bld.psum[bank][:, 0:512], 1.0, elu_t,
                    ALU.add, ALU.max, waits=((ac, ac.n),))
            bld.guard_bank(bank, dv, dv.n)
        sq_done = (dv, dv.n)
        combT_tiles = [ar.alloc(128, 512, BF16) for _ in range(FC)]
        comb_f = [ar.alloc(128, 2048, F32) for _ in range(2)]
        comb_b = [ar.alloc(128, 2048, BF16) for _ in range(2)]
        rn = ar.alloc(128, 4, F32)
        comb_guard = {}
        for sc in range(4):
            cf = comb_f[sc % 2]
            cb = comb_b[sc % 2]
            cg = comb_guard.get(sc % 2, 0)
            for h in range(H):
                ui = sc * H + h
                bank_a = (ui % 3) * 2
                bank_n = bank_a + 1
                w0 = bld.use_bank(bank_n, (sq_done, qm2))
                bld.mm(bld.psum[bank_n][:, 0:4], sq[h][:, sc * 128:(sc + 1) * 128],
                       mn_hn[:, 4 * h:4 * h + 4], True, True, waits=w0, inc=mm)
                mm_norm = mm.n
                wa0 = bld.use_bank(bank_a, (qm,))
                for n in range(NB):
                    bld.mm(bld.psum[bank_a][:, n * 128:(n + 1) * 128],
                           sq[h][:, sc * 128:(sc + 1) * 128],
                           mem_bf[:, (n * 16 + h) * 128:(n * 16 + h) * 128 + 128],
                           True, True, waits=wa0 if n == 0 else (), inc=mm)
                mm_a = mm.n
                bld.ts("vector", rn, bld.psum[bank_n][:, 0:4], 1e-6, 1.0, ALU.max, ALU.mult,
                       waits=((mm, mm_norm), prep_done))
                bld.recip(rn, rn)
                bld.tt("vector", rn, rn, wa_b[:, 4 * h:4 * h + 4], ALU.mult)
                bld.guard_bank(bank_n, dv, dv.n)
                bld.ts("vector", cf[:, h * 128:(h + 1) * 128],
                       bld.psum[bank_a][:, 0:128], rn[:, 0:1], 0.0, ALU.mult, ALU.add,
                       waits=((mm, mm_a), (dv, cg)))
                for n in range(1, NB):
                    bld.stt("vector", cf[:, h * 128:(h + 1) * 128],
                            bld.psum[bank_a][:, n * 128:(n + 1) * 128], rn[:, n:n + 1],
                            cf[:, h * 128:(h + 1) * 128], ALU.mult, ALU.add)
                bld.guard_bank(bank_a, dv, dv.n)
                bld.cp("vector", cb[:, h * 128:(h + 1) * 128], cf[:, h * 128:(h + 1) * 128])
            comb_guard[sc % 2] = dv.n
            for t in range(H):
                bank_t = 6 + (t % 2)
                w0 = bld.use_bank(bank_t, ((dv, comb_guard[sc % 2]), cm.ready))
                bld.transp(bld.psum[bank_t][:, 0:64].bitcast(BF16),
                           cb[:, t * 128:(t + 1) * 128], cm.ident_bf[:], waits=w0, inc=mm)
                bld.cp("vector", combT_tiles[t][:, sc * 128:sc * 128 + 128],
                       bld.psum[bank_t][:, 0:64].bitcast(BF16), waits=((mm, mm.n),))
                bld.guard_bank(bank_t, dv, dv.n)
        ct_done = (dv, dv.n)
        wo = [ar.alloc(128, 2048, BF16) for _ in range(2)]
        slot_o = [bld.dma_slot(f"l3ao{i}_h{h2}", "gpsimd") for i in range(2)]
        mm_prev2 = {}
        for oc in range(FC):
            buf = wo[oc % 2]
            dep = mm_prev2.get(oc - 2, 0)
            qv = slot_o[oc % 2].fill(buf, ap3(hb["l3a_o"], oc * 128 * 2048,
                                              [[2048, 128], [1, 2048]]),
                                     waits=((mm, dep),))
            bank = oc % 2
            for kc in range(FC):
                w0 = bld.use_bank(bank, (qv, ct_done)) if kc == 0 else ()
                bld.mm(bld.psum[bank][:, 0:512], buf[:, kc * 128:(kc + 1) * 128],
                       combT_tiles[kc][:, 0:512],
                       kc == 0, kc == FC - 1, waits=w0, inc=mm if kc == FC - 1 else None)
            mm_prev2[oc] = mm.n
            bld.tt("vector", x_tiles[oc][:, sl], x_tiles[oc][:, sl],
                   bld.psum[bank][:, 0:512], ALU.add, waits=((mm, mm.n),))
            bld.guard_bank(bank, dv, dv.n)
        done = (dv, dv.n)
        ar.reset(mark)
    ar.reset(mark_outer)
    return done


def host_prep_l3attn(inputs):
    wq = tile_lhsT(np.ascontiguousarray(np.asarray(inputs["L3_attn_Wq"]).T))
    wo = tile_lhsT(np.ascontiguousarray(np.asarray(inputs["L3_attn_Wo"]).T))
    mem = np.ascontiguousarray(np.asarray(inputs["L3_attn_memories"]))
    mn = np.asarray(inputs["L3_attn_memory_norms"])
    mn_hn = np.ascontiguousarray(mn.transpose(2, 1, 0).reshape(128, 64))
    mn_nh = np.ascontiguousarray(mn.transpose(2, 0, 1).reshape(128, 64))
    bwt = np.ascontiguousarray(np.asarray(inputs["L3_attn_bank_weights"]))
    return {"l3a_q": wq, "l3a_o": wo, "l3a_mem": mem, "l3a_mn_hn": mn_hn,
            "l3a_mn_nh": mn_nh, "l3a_bw": bwt}



def emit_l0attn_a(bld, cm, ar, S, hb, x_tiles, x_ready):
    mm, dv, ac, gp = S["mm"], S["dv"], S["ac"], S["gp"]
    mark0 = ar.mark()
    cos = ar.alloc(128, 1024, F32)
    sin = ar.alloc(128, 1024, F32)
    swp = ar.alloc(128, 128, F32)
    s_c = bld.dma_slot("l0a_cos", "sync")
    qc = s_c.fill(cos, hb["c_cos"][:, :])
    s_s = bld.dma_slot("l0a_sin", "sync")
    qs_ = s_s.fill(sin, hb["c_sin"][:, :])
    s_w = bld.dma_slot("l0a_swp", "sync")
    qw_ = s_w.fill(swp, hb["c_swap"][:, :])
    done = x_ready
    for h2 in range(2):
        sl = slice(h2 * 512, (h2 + 1) * 512)
        mark = ar.mark()
        xn = [ar.alloc(128, 512, BF16) for _ in range(FC)]
        xs = [x_tiles[c][:, sl] for c in range(FC)]
        ln_done = ln_fmajor(bld, cm, ar, xs, xn, 512, S, x_ready=(done,))
        qsb = [ar.alloc(128, 512, F32) for _ in range(2)]
        qss = [ar.alloc(128, 512, F32) for _ in range(2)]
        ropet = [ar.alloc(128, 512, F32) for _ in range(2)]
        routs = [ar.alloc(128, 512, BF16) for _ in range(2)]
        wbuf = [ar.alloc(128, 2048, BF16) for _ in range(2)]
        ev_prev = {}
        st_prev = {}
        for qk, wname, outname in ((0, "l0a_q", "q_rope"), (1, "l0a_k", "k_rope")):
            slot = [bld.dma_slot(f"l0a{qk}{i}_h{h2}", "gpsimd") for i in range(2)]
            mm_prev = {-2: mm.n, -1: mm.n}
            sst = [bld.dma_slot(f"l0as{qk}{i}_h{h2}", "sync") for i in range(2)]
            for mc in range(FC):
                ii = qk * FC + mc
                buf = wbuf[mc % 2]
                dep = mm_prev.get(mc - 2, 0)
                qv = slot[mc % 2].fill(buf, ap3(hb[wname], mc * 128 * 2048,
                                                [[2048, 128], [1, 2048]]),
                                       waits=((mm, dep),))
                bank = mc % 2
                for kc in range(FC):
                    w0 = bld.use_bank(bank, (qv, ln_done)) if kc == 0 else ()
                    bld.mm(bld.psum[bank][:, 0:512], buf[:, kc * 128:(kc + 1) * 128],
                           xn[kc], kc == 0, kc == FC - 1, waits=w0,
                           inc=mm if kc == FC - 1 else None)
                mm_prev[mc] = mm.n
                q = qsb[mc % 2]
                gprev = ev_prev.get(ii - 2, ((dv, 0), (gp, 0)))
                bld.cp("vector", q, bld.psum[bank][:, 0:512],
                       waits=((mm, mm.n), gprev[0], gprev[1]))
                bld.guard_bank(bank, dv, dv.n)
                cp_done = (dv, dv.n)
                qs2 = qss[mc % 2]
                bank_w = 2 + (mc % 2)
                w0 = bld.use_bank(bank_w, (cp_done, qw_))
                bld.mm(bld.psum[bank_w][:, 0:512], swp[:], q, True, True, waits=w0, inc=mm)
                bld.cp("vector", qs2, bld.psum[bank_w][:, 0:512], waits=((mm, mm.n),))
                bld.guard_bank(bank_w, dv, dv.n)
                qs_done = (dv, dv.n)
                u = ropet[mc % 2]
                rout = routs[mc % 2]
                cs = cos[:, sl]
                sn = sin[:, sl]
                stp = st_prev.get(ii - 2, None)
                ww = (qc, qs_, cp_done, qs_done) + ((stp,) if stp else ())
                bld.tt("gpsimd", u, q, cs, ALU.mult, waits=ww)
                bld.tt("vector", rout, qs2, sn, ALU.mult, waits=ww)
                bld.emit("vector", None, waits=((gp, gp.n),))
                bld.tt("vector", rout, rout, u, ALU.add)
                ev_prev[ii] = ((dv, dv.n), (gp, gp.n))
                st_prev[ii] = sst[mc % 2].fill(
                    ap3(hb[outname], mc * 128 * 1024 + h2 * 512, [[1024, 128], [1, 512]]),
                    rout[:], waits=((dv, dv.n),))
        wv = [ar.alloc(128, 16 * 512, BF16) for _ in range(2)]
        vts = [ar.alloc(128, 512, BF16) for _ in range(2)]
        slot_v = [bld.dma_slot(f"l0av{i}_h{h2}", "gpsimd") for i in range(2)]
        sst_v = [bld.dma_slot(f"l0avs{i}_h{h2}", "sync") for i in range(2)]
        mm_prev = {-2: mm.n, -1: mm.n}
        ev_prev = {}
        st_prev = {}
        for oc4 in range(4):
            buf = wv[oc4 % 2]
            dep = mm_prev.get(oc4 - 2, 0)
            qv = slot_v[oc4 % 2].fill(
                buf, ap3(hb["l0a_vt"], oc4 * 512, [[2048, 128], [2048 * 128, 16], [1, 512]]),
                waits=((mm, dep),))
            for tc in range(4):
                ui = oc4 * 4 + tc
                bank = 2 + (ui % 2)
                for kc in range(FC):
                    w0 = bld.use_bank(bank, (qv, ln_done)) if kc == 0 else ()
                    bld.mm(bld.psum[bank][:, 0:512],
                           xn[kc][:, tc * 128:(tc + 1) * 128],
                           buf[:, kc * 512:(kc + 1) * 512], kc == 0, kc == FC - 1,
                           waits=w0, inc=mm if kc == FC - 1 else None)
                mm_prev[oc4] = mm.n
                vt = vts[ui % 2]
                stp = st_prev.get(ui - 2, None)
                ww = ((mm, mm.n),) + ((stp,) if stp else ())
                bld.cp("vector", vt, bld.psum[bank][:, 0:512], waits=ww)
                bld.guard_bank(bank, dv, dv.n)
                st_prev[ui] = sst_v[ui % 2].fill(
                    ap3(hb["vt"], (h2 * 512 + tc * 128) * 2048 + oc4 * 512,
                        [[2048, 128], [1, 512]]),
                    vt[:], waits=((dv, dv.n),))
        store_sems = [(s.sem, s.sem.n) for s in
                      [x for pair in (sst_v,) for x in pair]]
        for nm, sm in bld.sems.items():
            if nm.startswith("q_l0as") and sm.n > 0:
                store_sems.append((sm, sm.n))
        bld.wait_engines(store_sems)
        done = (dv, dv.n)
        ar.reset(mark)
    ar.reset(mark0)
    return done


def build_l0attn_a_program():
    extra = [("l0a_q", (16, 128, 16, 128), F32), ("l0a_k", (16, 128, 16, 128), F32),
             ("l0a_vt", (2048, 2048), F32), ("c_cos", (128, 1024), F32),
             ("c_sin", (128, 1024), F32), ("c_swap", (128, 128), F32)]
    outs = [("q_rope", (2048, 1024), BF16), ("k_rope", (2048, 1024), BF16),
            ("vt", (1024, 2048), BF16)]
    return build_stage_program("l0attn_a", extra, emit_l0attn_a, arena_kb=140,
                               extra_outs=outs)



def emit_l0attn_b(bld, cm, ar, S, hb, x_tiles, x_ready, core):
    mm, dv, ac, gp = S["mm"], S["dv"], S["ac"], S["gp"]
    sA, sB = strips_of(core)
    nA, nB = 4 * (sA + 1), 4 * (sB + 1)
    NK = nA + nB
    plan = [("A", j >= nA - 4, (j - (nA - 4)) if j >= nA - 4 else -1) for j in range(nA)] + \
           [("B", j >= nB - 4, (j - (nB - 4)) if j >= nB - 4 else -1) for j in range(nB)]
    mark0 = ar.mark()
    masks = ar.alloc(128, 4 * 512, BF16)
    s_mk = bld.dma_slot("l0b_mk", "gpsimd")
    qmk = s_mk.fill(masks, ap3(hb["c_masks"], 0, [[512, 128], [128 * 512, 4], [1, 512]]))
    att = [ar.alloc(128, 1024, BF16) for _ in range(FC)]
    qh = ar.alloc(128, 1024, BF16)
    kt = [ar.alloc(128, 128, BF16) for _ in range(2)]
    vt = [ar.alloc(128, 128, BF16) for _ in range(2)]
    s2 = [ar.alloc(128, 512, F32) for _ in range(2)]
    s4 = [ar.alloc(128, 512, BF16) for _ in range(2)]
    denrow = ar.alloc(1, 1024, F32)
    att_f = ar.alloc(128, 512, F32)
    slot_q = bld.dma_slot("l0b_q", "sync")
    slot_k = [bld.dma_slot(f"l0b_k{i}", "sync") for i in range(2)]
    slot_v = [bld.dma_slot(f"l0b_v{i}", "sync") for i in range(2)]
    done = x_ready
    s4_guard = {}
    s2_guard = {}
    kv_guard = {}
    mm_prev_head = 0
    for h in range(16):
        qv_q = slot_q.fill(qh, ap3(hb["q_rope"], h * 128 * 1024, [[1024, 128], [1, 1024]]),
                           waits=(done, (mm, mm_prev_head)) if h == 0 else
                                 ((mm, mm_prev_head),))
        first_of = {"A": True, "B": True}
        nleft = {"A": nA, "B": nB}
        for j, (st, isdiag, mk) in enumerate(plan):
            ktile = kt[j % 2]
            vtile = vt[j % 2]
            qv_k = slot_k[j % 2].fill(
                ktile, ap3(hb["k_stream"], h * 128 * (NK * 128) + j * 128,
                           [[NK * 128, 128], [1, 128]]),
                waits=((mm, kv_guard.get(j - 2, mm_prev_head)),))
            qv_v = slot_v[j % 2].fill(
                vtile, ap3(hb["vt_stream"], j * 128 * 2048 + h * 128,
                           [[2048, 128], [1, 128]]),
                waits=((mm, kv_guard.get(j - 2, mm_prev_head)),))
            bank_s = 4 + (j % 2)
            qoff = 0 if st == "A" else 512
            w0 = bld.use_bank(bank_s, (qv_k, qv_q))
            bld.mm(bld.psum[bank_s][:, 0:512], ktile[:], qh[:, qoff:qoff + 512],
                   True, True, waits=w0, inc=mm)
            mm_s = mm.n
            s2t = s2[j % 2]
            s4t = s4[j % 2]
            sg = s2_guard.get(j % 2)
            bld.act(s2t, bld.psum[bank_s][:, 0:512], AF.Square,
                    waits=((mm, mm_s),) + ((sg,) if sg else ()), inc=ac)
            bld.guard_bank(bank_s, ac, ac.n)
            ac_s2 = ac.n
            eng = "vector" if (j % 2 == 0) else "gpsimd"
            es = bld.eng_sem(eng)
            s4g = (mm, s4_guard.get(j % 2, 0))
            if isdiag:
                bld.tt(eng, s2t, s2t, masks[:, mk * 512:(mk + 1) * 512], ALU.mult,
                       waits=((ac, ac_s2), qmk, s4g))
                bld.tt(eng, s4t, s2t, s2t, ALU.mult)
            else:
                bld.tt(eng, s4t, s2t, s2t, ALU.mult, waits=((ac, ac_s2), s4g))
            s2_guard[j % 2] = (es, es.n)
            s4_done = (es, es.n)
            bank_d = 6 if st == "A" else 7
            bank_o = 0 if st == "A" else 1
            first = first_of[st]
            last = (nleft[st] == 1)
            w0 = bld.use_bank(bank_d, (s4_done, cm.ready)) if first else (s4_done,)
            bld.mm(bld.psum[bank_d][0:1, 0:512], cm.ones_col_bf[:], s4t[:],
                   first, last, waits=w0, inc=mm if last else None)
            w0 = bld.use_bank(bank_o, (s4_done, qv_v)) if first else (qv_v,)
            bld.mm(bld.psum[bank_o][:, 0:512], vtile[:], s4t[:],
                   first, last, waits=w0, inc=mm)
            kv_guard[j] = mm.n
            s4_guard[j % 2] = mm.n
            first_of[st] = False
            nleft[st] -= 1
        mm_heads = mm.n
        mm_prev_head = mm.n
        for st, bank_d, bank_o, qoff in (("A", 6, 0, 0), ("B", 7, 1, 512)):
            dslice = denrow[0:1, qoff:qoff + 512]
            bld.ts("vector", dslice, bld.psum[bank_d][0:1, 0:512],
                   1e-6, 1.0, ALU.max, ALU.mult, waits=((mm, mm_heads),))
            bld.recip(dslice, dslice)
            bld.guard_bank(bank_d, dv, dv.n)
            w0 = bld.use_bank(2, ((dv, dv.n),))
            bld.mm(bld.psum[2][:, 0:512], cm.ones_row_f[:], dslice, True, True,
                   waits=w0, inc=mm)
            mm_bc = mm.n
            bld.cp("vector", att_f, bld.psum[bank_o][:, 0:512], waits=((mm, mm_heads),))
            bld.guard_bank(bank_o, dv, dv.n)
            bld.tt("vector", att[h][:, qoff:qoff + 512],
                   att_f, bld.psum[2][:, 0:512], ALU.mult, waits=((mm, mm_bc),))
            bld.guard_bank(2, dv, dv.n)
    att_done = (dv, dv.n)
    wo = [ar.alloc(128, 2048, BF16) for _ in range(2)]
    slot_o = [bld.dma_slot(f"l0bo{i}", "gpsimd") for i in range(2)]
    mm_prev2 = {}
    for h2 in range(2):
        sl = slice(h2 * 512, (h2 + 1) * 512)
        for oc in range(FC):
            i = h2 * FC + oc
            buf = wo[i % 2]
            dep = mm_prev2.get(i - 2, 0)
            qv = slot_o[i % 2].fill(buf, ap3(hb["l0a_o"], oc * 128 * 2048,
                                             [[2048, 128], [1, 2048]]),
                                    waits=((mm, dep),))
            bank = i % 2
            for kc in range(FC):
                w0 = bld.use_bank(bank, (qv, att_done)) if kc == 0 else ()
                bld.mm(bld.psum[bank][:, 0:512], buf[:, kc * 128:(kc + 1) * 128],
                       att[kc][:, sl], kc == 0, kc == FC - 1, waits=w0,
                       inc=mm if kc == FC - 1 else None)
            mm_prev2[i] = mm.n
            bld.tt("vector", x_tiles[oc][:, sl], x_tiles[oc][:, sl],
                   bld.psum[bank][:, 0:512], ALU.add, waits=((mm, mm.n),))
            bld.guard_bank(bank, dv, dv.n)
    done = (dv, dv.n)
    ar.reset(mark0)
    return done


def build_l0attn_b_program(core):
    sA, sB = strips_of(core)
    NK = 4 * (sA + 1) + 4 * (sB + 1)
    extra = [("q_rope", (2048, 1024), BF16), ("k_stream", (2048, NK * 128), BF16),
             ("vt_stream", (NK * 128, 2048), BF16), ("c_masks", (4, 128, 512), BF16),
             ("l0a_o", (16, 128, 16, 128), F32)]
    emit = lambda bld, cm, ar, S, hb, xt, xr: emit_l0attn_b(bld, cm, ar, S, hb, xt, xr, core)
    return build_stage_program(f"l0attn_b{core}", extra, emit, arena_kb=140)


NKU_A, NKU_B = 32, 64
NKU = NKU_A + NKU_B


def emit_l0attn_b_uniform(bld, cm, ar, S, hb, x_tiles, x_ready):
    mm, dv, ac, gp = S["mm"], S["dv"], S["ac"], S["gp"]
    NK = NKU
    plan = [("A", j >= NKU_A - 4, (j - (NKU_A - 4)) if j >= NKU_A - 4 else -1)
            for j in range(NKU_A)] + \
           [("B", j >= NKU_B - 4, (j - (NKU_B - 4)) if j >= NKU_B - 4 else -1)
            for j in range(NKU_B)]
    nA, nB = NKU_A, NKU_B
    mark0 = ar.mark()
    masks = ar.alloc(128, 4 * 512, BF16)
    s_mk = bld.dma_slot("l0b_mk", "gpsimd")
    qmk = s_mk.fill(masks, ap3(hb["c_masks"], 0, [[512, 128], [128 * 512, 4], [1, 512]]))
    att = [ar.alloc(128, 1024, BF16) for _ in range(FC)]
    qh = ar.alloc(128, 1024, BF16)
    kt = [ar.alloc(128, 128, BF16) for _ in range(2)]
    vt = [ar.alloc(128, 128, BF16) for _ in range(2)]
    s2 = [ar.alloc(128, 512, F32) for _ in range(2)]
    s4 = [ar.alloc(128, 512, BF16) for _ in range(2)]
    denrow = ar.alloc(1, 1024, F32)
    att_f = ar.alloc(128, 512, F32)
    slot_q = bld.dma_slot("l0b_q", "sync")
    slot_k = [bld.dma_slot(f"l0b_k{i}", "sync") for i in range(2)]
    slot_v = [bld.dma_slot(f"l0b_v{i}", "sync") for i in range(2)]
    done = x_ready
    s4_guard = {}
    s2_guard = {}
    kv_guard = {}
    mm_prev_head = 0
    for h in range(16):
        qv_q = slot_q.fill(qh, ap3(hb["q_rope"], h * 128 * 1024, [[1024, 128], [1, 1024]]),
                           waits=(done, (mm, mm_prev_head)) if h == 0 else
                                 ((mm, mm_prev_head),))
        first_of = {"A": True, "B": True}
        nleft = {"A": nA, "B": nB}
        for j, (st, isdiag, mk) in enumerate(plan):
            ktile = kt[j % 2]
            vtile = vt[j % 2]
            qv_k = slot_k[j % 2].fill(
                ktile, ap3(hb["k_stream"], h * 128 * (NK * 128) + j * 128,
                           [[NK * 128, 128], [1, 128]]),
                waits=((mm, kv_guard.get(j - 2, mm_prev_head)),))
            qv_v = slot_v[j % 2].fill(
                vtile, ap3(hb["vt_stream"], j * 128 * 2048 + h * 128,
                           [[2048, 128], [1, 128]]),
                waits=((mm, kv_guard.get(j - 2, mm_prev_head)),))
            bank_s = 4 + (j % 2)
            qoff = 0 if st == "A" else 512
            w0 = bld.use_bank(bank_s, (qv_k, qv_q))
            bld.mm(bld.psum[bank_s][:, 0:512], ktile[:], qh[:, qoff:qoff + 512],
                   True, True, waits=w0, inc=mm)
            mm_s = mm.n
            s2t = s2[j % 2]
            s4t = s4[j % 2]
            sg = s2_guard.get(j % 2)
            bld.act(s2t, bld.psum[bank_s][:, 0:512], AF.Square,
                    waits=((mm, mm_s),) + ((sg,) if sg else ()), inc=ac)
            bld.guard_bank(bank_s, ac, ac.n)
            ac_s2 = ac.n
            eng = "vector" if (j % 2 == 0) else "gpsimd"
            es = bld.eng_sem(eng)
            s4g = (mm, s4_guard.get(j % 2, 0))
            if isdiag:
                bld.tt(eng, s2t, s2t, masks[:, mk * 512:(mk + 1) * 512], ALU.mult,
                       waits=((ac, ac_s2), qmk, s4g))
                bld.tt(eng, s4t, s2t, s2t, ALU.mult)
            else:
                bld.tt(eng, s4t, s2t, s2t, ALU.mult, waits=((ac, ac_s2), s4g))
            s2_guard[j % 2] = (es, es.n)
            s4_done = (es, es.n)
            bank_d = 6 if st == "A" else 7
            bank_o = 0 if st == "A" else 1
            first = first_of[st]
            last = (nleft[st] == 1)
            w0 = bld.use_bank(bank_d, (s4_done, cm.ready)) if first else (s4_done,)
            bld.mm(bld.psum[bank_d][0:1, 0:512], cm.ones_col_bf[:], s4t[:],
                   first, last, waits=w0, inc=mm if last else None)
            w0 = bld.use_bank(bank_o, (s4_done, qv_v)) if first else (qv_v,)
            bld.mm(bld.psum[bank_o][:, 0:512], vtile[:], s4t[:],
                   first, last, waits=w0, inc=mm)
            kv_guard[j] = mm.n
            s4_guard[j % 2] = mm.n
            first_of[st] = False
            nleft[st] -= 1
        mm_heads = mm.n
        mm_prev_head = mm.n
        for st, bank_d, bank_o, qoff in (("A", 6, 0, 0), ("B", 7, 1, 512)):
            dslice = denrow[0:1, qoff:qoff + 512]
            bld.ts("vector", dslice, bld.psum[bank_d][0:1, 0:512],
                   1e-6, 1.0, ALU.max, ALU.mult, waits=((mm, mm_heads),))
            bld.recip(dslice, dslice)
            bld.guard_bank(bank_d, dv, dv.n)
            w0 = bld.use_bank(2, ((dv, dv.n),))
            bld.mm(bld.psum[2][:, 0:512], cm.ones_row_f[:], dslice, True, True,
                   waits=w0, inc=mm)
            mm_bc = mm.n
            bld.cp("vector", att_f, bld.psum[bank_o][:, 0:512], waits=((mm, mm_heads),))
            bld.guard_bank(bank_o, dv, dv.n)
            bld.tt("vector", att[h][:, qoff:qoff + 512],
                   att_f, bld.psum[2][:, 0:512], ALU.mult, waits=((mm, mm_bc),))
            bld.guard_bank(2, dv, dv.n)
    att_done = (dv, dv.n)
    wo = [ar.alloc(128, 2048, BF16) for _ in range(2)]
    slot_o = [bld.dma_slot(f"l0bo{i}", "gpsimd") for i in range(2)]
    mm_prev2 = {}
    for h2 in range(2):
        sl = slice(h2 * 512, (h2 + 1) * 512)
        for oc in range(FC):
            i = h2 * FC + oc
            buf = wo[i % 2]
            dep = mm_prev2.get(i - 2, 0)
            qv = slot_o[i % 2].fill(buf, ap3(hb["l0a_o"], oc * 128 * 2048,
                                             [[2048, 128], [1, 2048]]),
                                    waits=((mm, dep),))
            bank = i % 2
            for kc in range(FC):
                w0 = bld.use_bank(bank, (qv, att_done)) if kc == 0 else ()
                bld.mm(bld.psum[bank][:, 0:512], buf[:, kc * 128:(kc + 1) * 128],
                       att[kc][:, sl], kc == 0, kc == FC - 1, waits=w0,
                       inc=mm if kc == FC - 1 else None)
            mm_prev2[i] = mm.n
            bld.tt("vector", x_tiles[oc][:, sl], x_tiles[oc][:, sl],
                   bld.psum[bank][:, 0:512], ALU.add, waits=((mm, mm.n),))
            bld.guard_bank(bank, dv, dv.n)
    done = (dv, dv.n)
    ar.reset(mark0)
    return done


def build_l0attn_bu_program():
    extra = [("q_rope", (2048, 1024), BF16), ("k_stream", (2048, NKU * 128), BF16),
             ("vt_stream", (NKU * 128, 2048), BF16), ("c_masks", (4, 128, 512), BF16),
             ("l0a_o", (16, 128, 16, 128), F32)]
    return build_stage_program("l0attn_bu", extra, emit_l0attn_b_uniform,
                               arena_kb=140)


def l0_streams_dense(core, K_full, VT_full):
    sA, sB = strips_of(core)
    ks = np.zeros((2048, NKU * 128), K_full.dtype)
    vs = np.zeros((NKU * 128, 2048), VT_full.dtype)
    for grp, s, base, width in (("A", sA, 0, NKU_A), ("B", sB, NKU_A, NKU_B)):
        n = 4 * (s + 1)
        tiles = list(range(n))
        slot_map = list(range(n - 4)) + list(range(width - 4, width))
        for t, slot in zip(tiles, slot_map):
            dst = base + slot
            ks[:, 128 * dst:128 * (dst + 1)] = K_full[:, 128 * t:128 * (t + 1)]
            vs[128 * dst:128 * (dst + 1), :] = VT_full[128 * t:128 * (t + 1), :]
    return np.ascontiguousarray(ks), np.ascontiguousarray(vs)


def host_prep_l0attn(inputs):
    wq = tile_lhsT(np.ascontiguousarray(np.asarray(inputs["L0_attn_Wq"]).T))
    wk = tile_lhsT(np.ascontiguousarray(np.asarray(inputs["L0_attn_Wk"]).T))
    wo = tile_lhsT(np.ascontiguousarray(np.asarray(inputs["L0_attn_Wo"]).T))
    wvt = np.ascontiguousarray(np.asarray(inputs["L0_attn_Wv"]).T)
    theta = (10000.0 ** (-np.arange(64, dtype=np.float64) * 2.0 / 128.0))
    swap = np.zeros((128, 128), np.float32)
    for i in range(128):
        swap[i, (i + 64) % 128] = 1.0
    shared = {"l0a_q": wq, "l0a_k": wk, "l0a_o": wo, "l0a_vt": wvt, "c_swap": swap}
    per = []
    for c in range(NCORES):
        a, b = strips_of(c)
        pos = np.concatenate([np.arange(a * SW, (a + 1) * SW),
                              np.arange(b * SW, (b + 1) * SW)]).astype(np.float64)
        ang = theta[:, None] * pos[None, :]
        cosv = np.cos(ang).astype(np.float32)
        sinv = np.sin(ang).astype(np.float32)
        per.append({"c_cos": np.concatenate([cosv, cosv], axis=0),
                    "c_sin": np.concatenate([-sinv, sinv], axis=0)})
    return shared, per


def l0attn_masks():
    m = np.zeros((4, 128, 512), np.float32)
    for d in range(4):
        for kk in range(128):
            m[d, kk, d * 128 + kk:] = 1.0
    return m.astype(BF)


def merge_strips(shards, axis):
    parts = [None] * NSTRIP
    for c in range(NCORES):
        a, b = strips_of(c)
        sh = shards[c]
        idx = [slice(None)] * sh.ndim
        idx[axis] = slice(0, SW)
        parts[a] = sh[tuple(idx)]
        idx[axis] = slice(SW, 2 * SW)
        parts[b] = sh[tuple(idx)]
    return np.concatenate(parts, axis=axis)


def l0_streams(core, K_full, VT_full):
    sA, sB = strips_of(core)
    tiles = list(range(4 * (sA + 1))) + list(range(4 * (sB + 1)))
    k_stream = np.concatenate([K_full[:, 128 * g:128 * (g + 1)] for g in tiles], axis=1)
    vt_stream = np.concatenate([VT_full[128 * g:128 * (g + 1), :] for g in tiles], axis=0)
    return np.ascontiguousarray(k_stream), np.ascontiguousarray(vt_stream)



def emit_l2attn_a(bld, cm, ar, S, hb, x_tiles, x_ready):
    mm, dv, ac, gp = S["mm"], S["dv"], S["ac"], S["gp"]
    mark0 = ar.mark()
    qsum = ar.alloc(128, 16, F32)
    red1 = ar.alloc(128, 1, F32)
    zacc = ar.alloc(128, 32, F32)
    bld.mset("vector", qsum, 0.0)
    bld.mset("vector", zacc, 0.0)
    done = x_ready
    st_sems = []
    for h2 in range(2):
        sl = slice(h2 * 512, (h2 + 1) * 512)
        mark = ar.mark()
        xn = [ar.alloc(128, 512, BF16) for _ in range(FC)]
        xs = [x_tiles[c][:, sl] for c in range(FC)]
        ln_done = ln_fmajor(bld, cm, ar, xs, xn, 512, S, x_ready=(done,))
        wq = [ar.alloc(128, 2048, BF16) for _ in range(2)]
        slot_q = [bld.dma_slot(f"l2aq{i}_h{h2}", "gpsimd") for i in range(2)]
        mm_prev = {-2: mm.n, -1: mm.n}
        rg = {}
        for mc in range(FC):
            buf = wq[mc % 2]
            qv = slot_q[mc % 2].fill(buf, ap3(hb["l2a_q"], mc * 128 * 2048,
                                              [[2048, 128], [1, 2048]]),
                                     waits=((mm, mm_prev[mc - 2]),))
            bank = mc % 2
            for kc in range(FC):
                w0 = bld.use_bank(bank, (qv, ln_done)) if kc == 0 else ()
                bld.mm(bld.psum[bank][:, 0:512], buf[:, kc * 128:(kc + 1) * 128], xn[kc],
                       kc == 0, kc == FC - 1, waits=w0, inc=mm if kc == FC - 1 else None)
            mm_prev[mc] = mm.n
            bld.red(red1, bld.psum[bank][:, 0:512], ALU.add, waits=((mm, mm.n),))
            bld.tt("vector", qsum[:, mc:mc + 1], qsum[:, mc:mc + 1], red1, ALU.add)
            bld.guard_bank(bank, dv, dv.n)
        kpT = [ar.alloc(128, 2048, BF16) for _ in range(4)]
        vT = [ar.alloc(128, 2048, BF16) for _ in range(4)]
        elu_s = ar.alloc(128, 512, F32)
        wsl = [ar.alloc(128, 16 * 512, BF16) for _ in range(2)]
        slot_w = [bld.dma_slot(f"l2aw{i}_h{h2}", "gpsimd") for i in range(2)]
        wprev = {-2: mm.n, -1: mm.n}
        for i4, (wname, dstl, do_elu) in enumerate(
                [("l2a_kp", kpT, True), ("l2a_vp", vT, False)]):
            for oc4 in range(4):
                ii = i4 * 4 + oc4
                buf = wsl[ii % 2]
                qv = slot_w[ii % 2].fill(
                    buf, ap3(hb[wname], oc4 * 512,
                             [[2048, 128], [2048 * 128, 16], [1, 512]]),
                    waits=((mm, wprev[ii - 2]),))
                for tc in range(4):
                    bank = 2 + ((ii * 4 + tc) % 2)
                    for kc in range(FC):
                        w0 = bld.use_bank(bank, (qv, ln_done)) if kc == 0 else ()
                        bld.mm(bld.psum[bank][:, 0:512],
                               xn[kc][:, tc * 128:(tc + 1) * 128],
                               buf[:, kc * 512:(kc + 1) * 512], kc == 0, kc == FC - 1,
                               waits=w0, inc=mm if kc == FC - 1 else None)
                    wprev[ii] = mm.n
                    dst = dstl[tc][:, oc4 * 512:(oc4 + 1) * 512]
                    if do_elu:
                        bld.ts("vector", elu_s, bld.psum[bank][:, 0:512], 0.0, 1.0,
                               ALU.min, ALU.mult, waits=((mm, mm.n),))
                        bld.act(elu_s, elu_s, AF.Exp, waits=((dv, dv.n),), inc=ac)
                        bld.stt("vector", dst, bld.psum[bank][:, 0:512], 1.0, elu_s,
                                ALU.add, ALU.max, waits=((ac, ac.n),))
                    else:
                        bld.cp("vector", dst, bld.psum[bank][:, 0:512],
                               waits=((mm, mm.n),))
                    bld.guard_bank(bank, dv, dv.n)
        kv_done = (dv, dv.n)
        sst = [bld.dma_slot(f"l2as{i}_h{h2}", "sync") for i in range(2)]
        kvt = [ar.alloc(128, 128, F32) for _ in range(2)]
        st_prev = {}
        for h in range(16):
            bank = 4 + (h % 2)
            w0 = bld.use_bank(bank, (kv_done,))
            for tc in range(4):
                bld.mm(bld.psum[bank][:, 0:128], kpT[tc][:, h * 128:(h + 1) * 128],
                       vT[tc][:, h * 128:(h + 1) * 128], tc == 0, tc == 3,
                       waits=w0 if tc == 0 else ())
            for tc in range(4):
                bld.mm(bld.psum[bank][:, 128:129], kpT[tc][:, h * 128:(h + 1) * 128],
                       cm.ones_col_bf[:], tc == 0, tc == 3,
                       inc=mm if tc == 3 else None)
            mm_kv = mm.n
            kv_sb = kvt[h % 2]
            stp = st_prev.get(h - 2, None)
            ww = ((mm, mm_kv),) + ((stp,) if stp else ())
            bld.cp("vector", kv_sb, bld.psum[bank][:, 0:128], waits=ww)
            bld.tt("vector", zacc[:, h2 * 16 + h:h2 * 16 + h + 1],
                   zacc[:, h2 * 16 + h:h2 * 16 + h + 1],
                   bld.psum[bank][:, 128:129], ALU.add)
            bld.guard_bank(bank, dv, dv.n)
            st_prev[h] = sst[h % 2].fill(
                ap3(hb["l2kv"], (h2 * 16 + h) * 128 * 128, [[128, 128], [1, 128]]),
                kv_sb, waits=((dv, dv.n),))
        st_sems.extend([(s.sem, s.sem.n) for s in sst])
        done = (dv, dv.n)
        bld.wait_engines(st_sems)
        ar.reset(mark)
    so = bld.dma_slot("l2a_zo", "sync")
    so.fill(hb["l2z"][:, :], zacc, waits=((dv, dv.n),))
    so2 = bld.dma_slot("l2a_qo", "sync")
    so2.fill(hb["l2qs"][:, :], qsum, waits=((dv, dv.n),))
    ar.reset(mark0)
    return done



def emit_l2attn_b(bld, cm, ar, S, hb, x_tiles, x_ready):
    mm, dv, ac, gp = S["mm"], S["dv"], S["ac"], S["gp"]
    H = 16
    mark0 = ar.mark()
    mem_bf = ar.alloc(128, 4 * H * 128, BF16)
    mn_hn = ar.alloc(128, 64, BF16)
    mn_nh = ar.alloc(128, 64, F32)
    gate_t = bld.nc_sbuf("l2_gate", [1, 16], F32)
    gate = gate_t[0:1, :]
    mask = ar.alloc(128, 128, BF16)
    s1 = bld.dma_slot("l2bm", "gpsimd")
    qm = s1.fill(mem_bf, ap3(hb["l2mem"], 0, [[128, 128], [16384, 4 * H], [1, 128]]))
    s2_ = bld.dma_slot("l2bm2", "gpsimd")
    qm2 = s2_.fill(mn_hn, hb["l2mn_hn"][:, :])
    s3 = bld.dma_slot("l2bm3", "sync")
    qm3 = s3.fill(mn_nh, hb["l2mn_nh"][:, :])
    s4_ = bld.dma_slot("l2bm4", "sync")
    qm4 = s4_.fill(gate, hb["l2gate"][:, :])
    s5 = bld.dma_slot("l2bm5", "sync")
    qm5 = s5.fill(mask, hb["c_mask128"][:, :])
    qsl = ar.alloc(128, 128, F32)
    s6 = bld.dma_slot("l2bm6", "sync")
    qm6 = s6.fill(qsl, hb["qs_all"][:, :])
    qtot = ar.alloc(128, 16, BF16)
    qtf = ar.alloc(128, 16, F32)
    bld.cp("vector", qtf, qsl[:, 0:16], waits=(qm6,))
    for c2 in range(1, NCORES):
        bld.tt("vector", qtf, qtf, qsl[:, c2 * 16:(c2 + 1) * 16], ALU.add)
    bld.cp("vector", qtot, qtf)
    qtot_done = (dv, dv.n)
    relrow = bld.nc_sbuf("l2_relrow", [1, 64], F32)
    wrow = bld.nc_sbuf("l2_wrow", [1, 64], F32)
    g1m = bld.nc_sbuf("l2_g1m", [1, 16], F32)
    mx16 = bld.nc_sbuf("l2_mx16", [1, 16], F32)
    frow = bld.nc_sbuf("l2_frow", [1, 4], F32)
    for h in range(H):
        w0 = bld.use_bank(5, (qtot_done, qm2)) if h == 0 else ()
        bld.mm(bld.psum[5][0:1, 4 * h:4 * h + 4], qtot[:, h:h + 1],
               mn_hn[:, 4 * h:4 * h + 4], h == 0, h == H - 1, waits=w0,
               inc=mm if h == H - 1 else None)
    bld.ts("vector", relrow[0:1, :], bld.psum[5][0:1, 0:64],
           float((128.0 ** -0.5) / 8192.0), 0.0, ALU.mult, ALU.add,
           waits=((mm, mm.n),))
    bld.guard_bank(5, dv, dv.n)
    red1 = ar.alloc(128, 1, F32)
    for m in range(4):
        bld.red(red1, mn_nh[:, m * 16:(m + 1) * 16], ALU.add, waits=(qm3, (mm, mm.n)))
        w0 = bld.use_bank(5, ((dv, dv.n), cm.ready))
        bld.mm(bld.psum[5][0:1, m:m + 1], cm.ones_col_f[:], red1, True, True,
               waits=w0, inc=mm)
    bld.ts("vector", frow[0:1, :], bld.psum[5][0:1, 0:4], 1e-6, 1.0, ALU.is_ge,
           ALU.mult, waits=((mm, mm.n),))
    bld.guard_bank(5, dv, dv.n)
    relm = [ap3(relrow, m, [[64, 1], [4, 16]]) for m in range(4)]
    wrm = [ap3(wrow, m, [[64, 1], [4, 16]]) for m in range(4)]
    m16 = mx16[0:1, :]
    bld.tt("vector", m16, relm[0], relm[1], ALU.max)
    bld.tt("vector", m16, m16, relm[2], ALU.max)
    bld.tt("vector", m16, m16, relm[3], ALU.max)
    for m in range(4):
        bld.tt("vector", relm[m], relm[m], m16, ALU.subtract)
    bld.act(relrow[0:1, :], relrow[0:1, :], AF.Exp, waits=((dv, dv.n),), inc=ac)
    for m in range(4):
        bld.ts("vector", relm[m], relm[m], frow[0:1, m:m + 1], 0.0, ALU.mult, ALU.add,
               waits=((ac, ac.n),))
    bld.tt("vector", m16, relm[0], relm[1], ALU.add)
    bld.tt("vector", m16, m16, relm[2], ALU.add)
    bld.tt("vector", m16, m16, relm[3], ALU.add)
    bld.ts("vector", m16, m16, 1e-30, 1.0, ALU.add, ALU.mult)
    bld.recip(m16, m16)
    bld.act(gate, gate, AF.Sigmoid, waits=(qm4,), inc=ac)
    bld.tt("vector", m16, m16, gate, ALU.mult, waits=((ac, ac.n),))
    for m in range(4):
        bld.tt("vector", wrm[m], relm[m], m16, ALU.mult)
    bld.ts("vector", g1m[0:1, :], gate, -1.0, 1.0, ALU.mult, ALU.add)
    prep1 = (dv, dv.n)
    wa_gb = ar.alloc(128, 4 * H, F32)
    g1mc = ar.alloc(128, H, F32)
    for h in range(H):
        w0 = bld.use_bank(5, (prep1,) if h == 0 else ((dv, dv.n),))
        bld.mm(bld.psum[5][:, 0:4], cm.ones_row_f[:], wrow[0:1, 4 * h:4 * h + 4],
               True, True, waits=w0, inc=mm)
        bld.cp("vector", wa_gb[:, 4 * h:4 * h + 4], bld.psum[5][:, 0:4],
               waits=((mm, mm.n),))
        bld.guard_bank(5, dv, dv.n)
        w0 = bld.use_bank(5, ((dv, dv.n),))
        bld.mm(bld.psum[5][:, 0:1], cm.ones_row_f[:], g1m[0:1, h:h + 1], True, True,
               waits=w0, inc=mm)
        bld.cp("vector", g1mc[:, h:h + 1], bld.psum[5][:, 0:1], waits=((mm, mm.n),))
        bld.guard_bank(5, dv, dv.n)
    S_A = [ar.alloc(128, 128, BF16) for _ in range(H)]
    S_B = [ar.alloc(128, 128, BF16) for _ in range(H)]
    Z_AB = ar.alloc(128, 2 * H, F32)
    mark_states = ar.mark()
    S_Af = [ar.alloc(128, 128, F32) for _ in range(H)]
    S_Bf = [ar.alloc(128, 128, F32) for _ in range(H)]
    selab = ar.alloc(128, 32, F32)
    s8 = bld.dma_slot("l2bsel", "sync")
    qsel = s8.fill(selab, hb["l2sel"][:, :])
    kvload = [ar.alloc(128, H * 128, F32) for _ in range(2)]
    zall = ar.alloc(128, 256, F32)
    s7 = bld.dma_slot("l2bz", "sync")
    qm7 = s7.fill(zall, hb["z_all"][:, :])
    skv = [bld.dma_slot(f"l2bkv{i}", "sync") for i in range(2)]
    for h in range(H):
        bld.mset("vector", S_Af[h], 0.0)
        bld.mset("vector", S_Bf[h], 0.0)
    bld.mset("vector", Z_AB, 0.0)
    ld_guard = {}
    for s in range(NSTRIP - 1):
        buf = kvload[s % 2]
        qv = skv[s % 2].fill(buf, ap3(hb["kv_all"], s * H * 128 * 128,
                                      [[128, 128], [16384, H], [1, 128]]),
                             waits=((dv, ld_guard.get(s - 2, 0)),))
        for h in range(H):
            bld.stt("vector", S_Af[h], buf[:, h * 128:(h + 1) * 128],
                    selab[:, s:s + 1], S_Af[h], ALU.mult, ALU.add,
                    waits=(qv, qsel))
            bld.stt("vector", S_Bf[h], buf[:, h * 128:(h + 1) * 128],
                    selab[:, 16 + s:16 + s + 1], S_Bf[h], ALU.mult, ALU.add)
        bld.stt("vector", Z_AB[:, 0:H], zall[:, s * 16:(s + 1) * 16],
                selab[:, s:s + 1], Z_AB[:, 0:H], ALU.mult, ALU.add, waits=(qm7,))
        bld.stt("vector", Z_AB[:, H:2 * H], zall[:, s * 16:(s + 1) * 16],
                selab[:, 16 + s:16 + s + 1], Z_AB[:, H:2 * H], ALU.mult, ALU.add)
        ld_guard[s] = dv.n
    for h in range(H):
        bld.cp("vector", S_A[h], S_Af[h])
        bld.cp("vector", S_B[h], S_Bf[h])
    states_done = (dv, dv.n)
    ar.reset(mark_states)
    done = x_ready
    for h2 in range(2):
        sl = slice(h2 * 512, (h2 + 1) * 512)
        S_in = S_A if h2 == 0 else S_B
        zoff = 0 if h2 == 0 else H
        mark = ar.mark()
        combT_tiles = [ar.alloc(128, 512, BF16) for _ in range(FC)]
        qp = [ar.alloc(128, 512, BF16) for _ in range(H)]
        kp = [ar.alloc(128, 512, BF16) for _ in range(H)]
        kpT = [ar.alloc(128, 2048, BF16) for _ in range(4)]
        vT = [ar.alloc(128, 2048, BF16) for _ in range(4)]
        mark_x = ar.mark()
        xn = [ar.alloc(128, 512, BF16) for _ in range(FC)]
        xs = [x_tiles[c][:, sl] for c in range(FC)]
        mark_ln = ar.mark()
        ln_done = ln_fmajor(bld, cm, ar, xs, xn, 512, S, x_ready=(done,))
        ar.reset(mark_ln)
        bld.emit("gpsimd", None, waits=(ln_done,))
        bld.emit("sync", None, waits=(ln_done,))
        elu_s = ar.alloc(128, 512, F32)
        wslab = [ar.alloc(128, 2048, BF16) for _ in range(2)]
        slot_w = [bld.dma_slot(f"l2bw{i}_h{h2}", "gpsimd") for i in range(2)]
        mm_prev = {-2: mm.n, -1: mm.n}
        for i2, (wname, dst) in enumerate([("l2a_qt", qp), ("l2a_kt", kp)]):
            for mc in range(FC):
                ii = i2 * FC + mc
                buf = wslab[ii % 2]
                qv = slot_w[ii % 2].fill(buf, ap3(hb[wname], mc * 128 * 2048,
                                                  [[2048, 128], [1, 2048]]),
                                         waits=((mm, mm_prev[ii - 2]),))
                bank = ii % 2
                for kc in range(FC):
                    w0 = bld.use_bank(bank, (qv, ln_done)) if kc == 0 else ()
                    bld.mm(bld.psum[bank][:, 0:512], buf[:, kc * 128:(kc + 1) * 128],
                           xn[kc], kc == 0, kc == FC - 1, waits=w0,
                           inc=mm if kc == FC - 1 else None)
                mm_prev[ii] = mm.n
                bld.ts("vector", elu_s, bld.psum[bank][:, 0:512], 0.0, 1.0,
                       ALU.min, ALU.mult, waits=((mm, mm.n),))
                bld.act(elu_s, elu_s, AF.Exp, waits=((dv, dv.n),), inc=ac)
                bld.stt("vector", dst[mc], bld.psum[bank][:, 0:512], 1.0, elu_s,
                        ALU.add, ALU.max, waits=((ac, ac.n),))
                bld.guard_bank(bank, dv, dv.n)
        wsl2 = [ar.alloc(128, 4 * 512, BF16) for _ in range(2)]
        slot_w2 = [bld.dma_slot(f"l2bw2{i}_h{h2}", "gpsimd") for i in range(2)]
        wprev = {-4: mm.n, -3: mm.n, -2: mm.n, -1: mm.n}
        for i4, (wname, dstl, do_elu) in enumerate(
                [("l2a_kp", kpT, True), ("l2a_vp", vT, False)]):
            for oc4 in range(4):
                for half8 in range(4):
                    ii = (i4 * 4 + oc4) * 4 + half8
                    buf = wsl2[ii % 2]
                    qv = slot_w2[ii % 2].fill(
                        buf, ap3(hb[wname], half8 * 4 * 128 * 2048 + oc4 * 512,
                                 [[2048, 128], [2048 * 128, 4], [1, 512]]),
                        waits=((mm, wprev[ii - 2]),))
                    for tc in range(4):
                        bank = 2 + tc
                        for kc in range(4):
                            kcg = half8 * 4 + kc
                            if kc == 0 and half8 == 0:
                                w0 = bld.use_bank(bank, (qv, ln_done))
                            elif kc == 0:
                                w0 = (qv,)
                            else:
                                w0 = ()
                            bld.mm(bld.psum[bank][:, 0:512],
                                   xn[kcg][:, tc * 128:(tc + 1) * 128],
                                   buf[:, kc * 512:(kc + 1) * 512],
                                   (half8 == 0 and kc == 0), (half8 == 3 and kc == 3),
                                   waits=w0, inc=mm if kc == 3 else None)
                        wprev[ii] = mm.n
                        if half8 == 3:
                            dst = dstl[tc][:, oc4 * 512:(oc4 + 1) * 512]
                            if do_elu:
                                bld.ts("vector", elu_s, bld.psum[bank][:, 0:512],
                                       0.0, 1.0, ALU.min, ALU.mult,
                                       waits=((mm, mm.n),))
                                bld.act(elu_s, elu_s, AF.Exp, waits=((dv, dv.n),),
                                        inc=ac)
                                bld.stt("vector", dst, bld.psum[bank][:, 0:512], 1.0,
                                        elu_s, ALU.add, ALU.max, waits=((ac, ac.n),))
                            else:
                                bld.cp("vector", dst, bld.psum[bank][:, 0:512],
                                       waits=((mm, mm.n),))
                            bld.guard_bank(bank, dv, dv.n)
        proj_done = (dv, dv.n)
        ar.reset(mark_x)
        bld.emit("gpsimd", None, waits=(proj_done,))
        bld.emit("sync", None, waits=(proj_done,))
        S_cur = [ar.alloc(128, 128, F32) for _ in range(H)]
        S_bf = [ar.alloc(128, 128, BF16) for _ in range(H)]
        Z_cur = ar.alloc(128, H, F32)
        Z_bf = ar.alloc(128, H, BF16)
        for h in range(H):
            bld.cp("vector", S_cur[h], S_in[h], waits=(states_done,))
            bld.cp("vector", S_bf[h], S_in[h])
        bld.cp("vector", Z_cur, Z_AB[:, zoff:zoff + H])
        bld.cp("vector", Z_bf, Z_AB[:, zoff:zoff + H])
        comb_f = ar.alloc(128, 2048, F32)
        comb_b = ar.alloc(128, 2048, BF16)
        msT = ar.alloc(128, 128, BF16)
        rd2 = ar.alloc(128, 1, F32)
        rn = ar.alloc(128, 4, F32)
        comb_guard = 0
        for cchunk in range(4):
            cf = comb_f
            cb = comb_b
            cg = comb_guard
            csl = slice(cchunk * 128, (cchunk + 1) * 128)
            for h in range(H):
                hs = slice(h * 128, (h + 1) * 128)
                w0 = bld.use_bank(0, (proj_done, qm5))
                bld.mm(bld.psum[0][:, 0:128], kp[h][:, csl], qp[h][:, csl],
                       True, True, waits=w0, inc=mm)
                bld.tt("vector", msT, bld.psum[0][:, 0:128], mask, ALU.mult,
                       waits=((mm, mm.n),))
                bld.guard_bank(0, dv, dv.n)
                ms_done = (dv, dv.n)
                w0 = bld.use_bank(1, (ms_done,))
                bld.mm(bld.psum[1][:, 0:128], msT, vT[cchunk][:, hs], True, False,
                       waits=w0)
                bld.mm(bld.psum[1][:, 0:128], qp[h][:, csl], S_bf[h], False, True,
                       inc=mm)
                mm_num = mm.n
                w0 = bld.use_bank(2, (ms_done,))
                bld.mm(bld.psum[2][:, 0:1], msT, cm.ones_col_bf[:], True, False,
                       waits=w0)
                bld.mm(bld.psum[2][:, 0:1], qp[h][:, csl], Z_bf[:, h:h + 1], False,
                       True)
                bld.mm(bld.psum[2][:, 4:8], qp[h][:, csl], mn_hn[:, 4 * h:4 * h + 4],
                       True, True, waits=(qm2,), inc=mm)
                mm_den = mm.n
                w0 = bld.use_bank(3, (qm,))
                for n in range(4):
                    bld.mm(bld.psum[3][:, n * 128:(n + 1) * 128], qp[h][:, csl],
                           mem_bf[:, (n * 16 + h) * 128:(n * 16 + h + 1) * 128],
                           True, True, waits=w0 if n == 0 else (), inc=mm)
                mm_a = mm.n
                bld.ts("vector", rd2, bld.psum[2][:, 0:1], 1e-6, 1.0, ALU.max,
                       ALU.mult, waits=((mm, mm_den),))
                bld.recip(rd2, rd2)
                bld.tt("vector", rd2, rd2, g1mc[:, h:h + 1], ALU.mult)
                bld.ts("vector", rn, bld.psum[2][:, 4:8], 1e-6, 1.0, ALU.max,
                       ALU.mult)
                bld.recip(rn, rn)
                bld.tt("vector", rn, rn, wa_gb[:, 4 * h:4 * h + 4], ALU.mult)
                bld.guard_bank(2, dv, dv.n)
                bld.ts("vector", cf[:, hs], bld.psum[1][:, 0:128], rd2[:, 0:1], 0.0,
                       ALU.mult, ALU.add, waits=((mm, mm_num), (dv, cg)))
                bld.guard_bank(1, dv, dv.n)
                for n in range(4):
                    bld.stt("vector", cf[:, hs], bld.psum[3][:, n * 128:(n + 1) * 128],
                            rn[:, n:n + 1], cf[:, hs], ALU.mult, ALU.add,
                            waits=((mm, mm_a),) if n == 0 else ())
                bld.guard_bank(3, dv, dv.n)
                bld.cp("vector", cb[:, hs], cf[:, hs])
                w0 = bld.use_bank(4, ())
                bld.mm(bld.psum[4][:, 0:128], kpT[cchunk][:, hs], vT[cchunk][:, hs],
                       True, True, waits=w0)
                bld.mm(bld.psum[4][:, 128:129], kpT[cchunk][:, hs], cm.ones_col_bf[:],
                       True, True, inc=mm)
                mm_kv = mm.n
                bld.tt("vector", S_cur[h], S_cur[h], bld.psum[4][:, 0:128], ALU.add,
                       waits=((mm, mm_kv),))
                bld.cp("vector", S_bf[h], S_cur[h])
                bld.tt("vector", Z_cur[:, h:h + 1], Z_cur[:, h:h + 1],
                       bld.psum[4][:, 128:129], ALU.add)
                bld.cp("vector", Z_bf[:, h:h + 1], Z_cur[:, h:h + 1])
                bld.guard_bank(4, dv, dv.n)
            comb_guard = dv.n
            for t in range(H):
                bank_t = 6 + (t % 2)
                w0 = bld.use_bank(bank_t, ((dv, comb_guard), cm.ready))
                bld.transp(bld.psum[bank_t][:, 0:64].bitcast(BF16),
                           cb[:, t * 128:(t + 1) * 128], cm.ident_bf[:], waits=w0,
                           inc=mm)
                bld.cp("vector", combT_tiles[t][:, cchunk * 128:cchunk * 128 + 128],
                       bld.psum[bank_t][:, 0:64].bitcast(BF16), waits=((mm, mm.n),))
                bld.guard_bank(bank_t, dv, dv.n)
            comb_guard = dv.n
        ct_done = (dv, dv.n)
        wo = [ar.alloc(128, 2048, BF16) for _ in range(2)]
        slot_o = [bld.dma_slot(f"l2bo{i}_h{h2}", "gpsimd") for i in range(2)]
        mm_prev2 = {-2: mm.n, -1: mm.n}
        for oc in range(FC):
            buf = wo[oc % 2]
            qv = slot_o[oc % 2].fill(buf, ap3(hb["l2a_ot"], oc * 128 * 2048,
                                              [[2048, 128], [1, 2048]]),
                                     waits=((mm, mm_prev2[oc - 2]),))
            bank = oc % 2
            for kc in range(FC):
                w0 = bld.use_bank(bank, (qv, ct_done)) if kc == 0 else ()
                bld.mm(bld.psum[bank][:, 0:512], buf[:, kc * 128:(kc + 1) * 128],
                       combT_tiles[kc][:, 0:512], kc == 0, kc == FC - 1, waits=w0,
                       inc=mm if kc == FC - 1 else None)
            mm_prev2[oc] = mm.n
            bld.tt("vector", x_tiles[oc][:, sl], x_tiles[oc][:, sl],
                   bld.psum[bank][:, 0:512], ALU.add, waits=((mm, mm.n),))
            bld.guard_bank(bank, dv, dv.n)
        done = (dv, dv.n)
        ar.reset(mark)
    ar.reset(mark0)
    return done


def host_prep_l2attn(inputs):
    wq = np.asarray(inputs["L2_attn_w_q"])
    wk = np.asarray(inputs["L2_attn_w_k"])
    wv = np.asarray(inputs["L2_attn_w_v"])
    wo = np.asarray(inputs["L2_attn_w_o"])
    mem = np.ascontiguousarray(np.asarray(inputs["L2_attn_memories"]))
    mn = np.asarray(inputs["L2_attn_memory_norms"])
    out = {
        "l2a_q": tile_lhsT(np.ascontiguousarray(wq.T)),
        "l2a_qt": tile_lhsT(np.ascontiguousarray(wq.T)),
        "l2a_kt": tile_lhsT(np.ascontiguousarray(wk.T)),
        "l2a_ot": tile_lhsT(np.ascontiguousarray(wo.T)),
        "l2a_kp": np.ascontiguousarray(wk.T),
        "l2a_vp": np.ascontiguousarray(wv.T),
        "l2mem": mem,
        "l2mn_hn": np.ascontiguousarray(mn.transpose(2, 1, 0).reshape(128, 64)),
        "l2mn_nh": np.ascontiguousarray(mn.transpose(2, 0, 1).reshape(128, 64)),
        "l2gate": np.ascontiguousarray(np.asarray(inputs["L2_attn_gate"]).reshape(1, 16)),
        "c_mask128": np.triu(np.ones((128, 128), np.float32)).astype(BF),
    }
    return out



L1_C = 64


def _l1_phase_a(bld, cm, ar, S, hb, x_tiles, x_ready, h2, bnd_col, outs,
                 skip_q=False):
    mm, dv, ac, gp = S["mm"], S["dv"], S["ac"], S["gp"]
    sl = slice(h2 * 512, (h2 + 1) * 512)
    xn = outs["xn"]
    xs = [x_tiles[c][:, sl] for c in range(FC)]
    mark_ln = ar.mark()
    ln_done = ln_fmajor(bld, cm, ar, xs, xn, 512, S, x_ready=(x_ready,))
    ar.reset(mark_ln)
    bld.emit("gpsimd", None, waits=(ln_done,))
    bld.emit("sync", None, waits=(ln_done,))
    maa = hb["_maa_sb"]
    maa_rdy = hb["_maa_rdy"]
    dx = [ar.alloc(128, 512, BF16) for _ in range(FC)]
    t32 = [ar.alloc(128, 512, F32) for _ in range(2)]
    mm_t = ar.alloc(128, 512, BF16)
    mm_sw = ar.alloc(128, 512, BF16)
    mark_mix = ar.mark()
    xxx = outs.get("xxx_bufs") or [ar.alloc(128, 512, BF16) for _ in range(FC)]
    for c in range(FC):
        bld.tt("vector", dx[c][:, 1:512], xn[c][:, 0:511], xn[c][:, 1:512],
               ALU.subtract, waits=(ln_done,))
        if bnd_col is None:
            bld.ts("vector", dx[c][:, 0:1], xn[c][:, 0:1], -1.0, 0.0, ALU.mult,
                   ALU.add)
        else:
            bld.tt("vector", dx[c][:, 0:1], bnd_col[:, c:c + 1], xn[c][:, 0:1],
                   ALU.subtract, waits=(hb["_bnd_rdy"],))
        bld.stt("vector", t32[0], dx[c], maa[:, c:c + 1], xn[c], ALU.mult, ALU.add,
                waits=(maa_rdy,))
        bld.cp("vector", xxx[c], t32[0])
    mix_done = (dv, dv.n)
    w1b = outs.get("w1_bufs")
    if w1b is None:
        w1 = ar.alloc(128, 2048, BF16)
        w1b = [w1[:, i * 512:(i + 1) * 512] for i in range(4)]
    s_w1 = bld.dma_slot(f"l1w1_h{h2}{hb['_tag']}", "gpsimd")
    qw1 = None
    for i in range(4):
        qw1 = s_w1.fill(w1b[i][0:128, 0:512],
                        ap3(hb["l1_w1t"], i * 512, [[2048, 128], [1, 512]]))
    for kc in range(FC):
        w0 = bld.use_bank(0, (qw1, mix_done)) if kc == 0 else ()
        bld.mm(bld.psum[0][:, 0:512],
               w1b[kc // 4][0:128, (kc % 4) * 128:(kc % 4) * 128 + 128], xxx[kc],
               kc == 0, kc == FC - 1, waits=w0, inc=mm if kc == FC - 1 else None)
    bld.act(mm_t, bld.psum[0][:, 0:512], AF.Tanh, waits=((mm, mm.n),), inc=ac)
    bld.guard_bank(0, ac, ac.n)
    swp = hb["_swp_sb"]
    w0 = bld.use_bank(1, ((ac, ac.n), hb["_swp_rdy"]))
    bld.mm(bld.psum[1][:, 0:512], swp[:], mm_t, True, True, waits=w0, inc=mm)
    bld.cp("vector", mm_sw, bld.psum[1][:, 0:512], waits=((mm, mm.n),))
    bld.guard_bank(1, dv, dv.n)
    mmt_done = ((ac, ac.n), (dv, dv.n))
    ar.reset(mark_mix)
    bld.emit("gpsimd", None, waits=mmt_done)
    bld.emit("sync", None, waits=mmt_done)
    xcur = [ar.alloc(128, 512, BF16) for _ in range(FC)]
    vt_st = [ar.alloc(64, 512, BF16) for _ in range(2)]
    vt_sts = [bld.dma_slot(f"l1vt{i}_h{h2}{hb['_tag']}", "sync") for i in range(2)]
    vt_prev = {}
    vt_stored = []
    wsl = [ar.alloc(128, 2048, BF16) for _ in range(2)]
    slot_w = [bld.dma_slot(f"l1pw{i}_h{h2}{hb['_tag']}", "gpsimd") for i in range(2)]
    w2t = [ar.alloc(64, 128, BF16) for _ in range(2)]
    slot_w2 = [bld.dma_slot(f"l1w2_{i}_h{h2}{hb['_tag']}", "gpsimd") for i in range(2)]
    w2dep = {-2: 0, -1: 0}
    mm_prevw = {-2: mm.n, -1: mm.n}
    w2 = hb["l1_w2"]
    rs = [1, 2, 3] if skip_q else [0, 1, 2, 3]
    for ri, r in enumerate(rs):
        rsrc = [mm_t, mm_t, mm_sw, mm_sw][r]
        rbase = [0, 32, 0, 32][r]
        maa_col_off = 16 * (1 + r)
        for oc in range(FC):
            i6 = ri * FC + oc
            bank = 2 + (oc % 2)
            qv = slot_w2[oc % 2].fill(
                w2t[oc % 2][rbase:rbase + 32, :],
                ap3(w2, r * 32 * 2048 + oc * 128, [[2048, 32], [1, 128]]),
                waits=((mm, w2dep[i6 - 2]),))
            w0 = bld.use_bank(bank, (qv, mmt_done[0], mmt_done[1]))
            bld.mm(bld.psum[bank][:, 0:512], w2t[oc % 2][rbase:rbase + 32, :],
                   rsrc[rbase:rbase + 32, :], True, True, waits=w0, inc=mm)
            w2dep[i6] = mm.n
            t1 = t32[oc % 2]
            bld.ts("vector", t1, bld.psum[bank][:, 0:512],
                   maa[:, maa_col_off + oc:maa_col_off + oc + 1], 0.0, ALU.add,
                   ALU.add, waits=((mm, mm.n),))
            bld.guard_bank(bank, dv, dv.n)
            bld.tt("vector", t1, t1, dx[oc], ALU.mult)
            bld.tt("vector", xcur[oc], t1, xn[oc], ALU.add)
        xcur_done = (dv, dv.n)
        if r in (0, 1):
            wname = "l1_qw" if r == 0 else "l1_kw"
            bcol = 0 if r == 0 else 16
            dst = outs["q"] if r == 0 else outs["k"]
            for mc in range(FC):
                i7 = ri * FC + mc
                buf = wsl[mc % 2]
                qv = slot_w[mc % 2].fill(buf, ap3(hb[wname], mc * 128 * 2048,
                                                  [[2048, 128], [1, 2048]]),
                                         waits=((mm, mm_prevw[i7 - 2]),))
                bank = 4 + (mc % 2)
                for kc in range(FC):
                    w0 = bld.use_bank(bank, (qv, xcur_done)) if kc == 0 else ()
                    bld.mm(bld.psum[bank][:, 0:512], buf[:, kc * 128:(kc + 1) * 128],
                           xcur[kc], kc == 0, kc == FC - 1, waits=w0,
                           inc=mm if kc == FC - 1 else None)
                mm_prevw[i7] = mm.n
                bld.ts("vector", dst[mc], bld.psum[bank][:, 0:512],
                       hb["_qkb_sb"][:, bcol + mc:bcol + mc + 1], 0.0, ALU.add,
                       ALU.add, waits=((mm, mm.n), hb["_qkb_rdy"]))
                bld.guard_bank(bank, dv, dv.n)
        elif r == 2:
            ibase = (1 if skip_q else 2) * FC
            for oc4 in range(4):
                for q4 in range(4):
                    i7 = ibase + oc4 * 4 + q4
                    buf = wsl[q4 % 2]
                    qv = slot_w[q4 % 2].fill(
                        buf, ap3(hb["l1_vwp"], q4 * 4 * 128 * 2048 + oc4 * 512,
                                 [[2048, 128], [2048 * 128, 4], [1, 512]]),
                        waits=((mm, mm_prevw[i7 - 2]),))
                    first_q4 = True
                    for kc in range(4):
                        kcg = q4 * 4 + kc
                        for tc8 in range(8):
                            if q4 == 0 and kc == 0:
                                w0 = bld.use_bank(tc8, (qv, xcur_done))
                            elif first_q4:
                                w0 = (qv,)
                            else:
                                w0 = ()
                            first_q4 = False
                            last = (kc == 3 and tc8 == 7)
                            bld.mm(bld.psum[tc8][0:64, 0:512],
                                   xcur[kcg][:, tc8 * 64:(tc8 + 1) * 64],
                                   buf[:, kc * 512:(kc + 1) * 512],
                                   (q4 == 0 and kc == 0), False, waits=w0,
                                   inc=mm if last else None)
                    mm_prevw[i7] = mm.n
                for tc8 in range(8):
                    bld.mm(bld.psum[tc8][0:64, 0:512], cm.ones_row_bf[0:1, 0:64],
                           hb["_vb_sb"][0:1, oc4 * 512:(oc4 + 1) * 512],
                           False, True, waits=(hb["_vb_rdy"],), inc=mm)
                    st = vt_st[tc8 % 2]
                    bld.cp("vector", st[0:64, 0:512], bld.psum[tc8][0:64, 0:512],
                           waits=((mm, mm.n), vt_prev.get(tc8 % 2, (mm, 0))))
                    bld.guard_bank(tc8, dv, dv.n)
                    vt_prev[tc8 % 2] = vt_sts[tc8 % 2].fill(
                        ap3(hb["l1vt"], tc8 * 64 * 2048 + oc4 * 512,
                            [[2048, 64], [1, 512]]),
                        st[0:64, 0:512], waits=((dv, dv.n),))
                    vt_stored.append(vt_prev[tc8 % 2])
        else:
            td1 = ar.alloc(128, 16 * 64, BF16)
            s_td = bld.dma_slot(f"l1td1_h{h2}{hb['_tag']}", "gpsimd")
            qtd = s_td.fill(td1, ap3(hb["l1_td1"], 0,
                                     [[64, 128], [128 * 64, 16], [1, 64]]))
            for kc in range(FC):
                w0 = bld.use_bank(4, (qtd, xcur_done)) if kc == 0 else ()
                bld.mm(bld.psum[4][0:64, 0:512], td1[:, kc * 64:(kc + 1) * 64],
                       xcur[kc], kc == 0, kc == FC - 1, waits=w0,
                       inc=mm if kc == FC - 1 else None)
            bld.act(outs["tanh_td"], bld.psum[4][0:64, 0:512], AF.Tanh,
                    waits=((mm, mm.n),), inc=ac)
            bld.guard_bank(4, ac, ac.n)
    return (dv, dv.n), (ac, ac.n), tuple(vt_stored[-2:]) if vt_stored else ()


def _l1_wl_chunk(bld, cm, ar, S, hb, tanh_td, cslice, wl_tiles, tag):
    mm, dv, ac = S["mm"], S["dv"], S["ac"]
    maa = hb["_maa_sb"]
    td2 = hb["_td2_sb"]
    for oc in range(FC):
        bank = 6 + (oc % 2)
        w0 = bld.use_bank(bank, (hb["_td2_rdy"], (ac, ac.n)))
        bld.mm(bld.psum[bank][0:128, 0:64], td2[:, oc * 128:(oc + 1) * 128],
               tanh_td[:, cslice], True, True, waits=w0, inc=mm)
        bld.act(wl_tiles[oc], bld.psum[bank][0:128, 0:64], AF.Exp,
                bias=maa[:, 80 + oc:80 + oc + 1], waits=((mm, mm.n), hb["_maa_rdy"]),
                inc=ac)
        bld.guard_bank(bank, ac, ac.n)
        bld.ts("vector", wl_tiles[oc], wl_tiles[oc], -1.0, -5.0, ALU.mult, ALU.max,
               waits=((ac, ac.n),))
    return (dv, dv.n)


def _l1_chunks(bld, cm, ar, S, hb, h2, q, k, vt_ready, tanh_td, S_t, S_bf,
               compute_out, y_hbm=None, Dacc=None):
    mm, dv, ac, gp = S["mm"], S["dv"], S["ac"], S["gp"]
    co = int(compute_out) if not isinstance(compute_out, bool) else \
        (3 if compute_out else 0)
    LNS = float(math.log(64.0 ** -0.5))
    mask = hb["_mask64_sb"]
    wl2 = [[ar.alloc(128, 64, F32) for _ in range(FC)] for _ in range(2)]
    bt = ar.alloc(128, 64, F32)
    biasA = ar.alloc(128, 1, F32)
    At = ar.alloc(128, 64, F32)
    emt = ar.alloc(128, 64, F32)
    ewl = ar.alloc(128, 64, F32)
    qt = [ar.alloc(128, 64, BF16) for _ in range(2)]
    kt = [ar.alloc(128, 64, BF16) for _ in range(2)]
    ktT = [ar.alloc(64, 128, BF16) for _ in range(2)]
    Dcol = ar.alloc(128, 1, F32)
    pm2 = [ar.alloc(128, 64, BF16) for _ in range(2)]
    vt2 = [ar.alloc(128, 2048, BF16) for _ in range(2)]
    vt_slots = [bld.dma_slot(f"l1vtl{i}_h{h2}{hb['_tag']}", "gpsimd")
                for i in range(2)]
    vt_rdy = {}
    vt_guard = {}
    yp = [bld.nc_sbuf(f"l1yp{i}_h{h2}{hb['_tag']}", [128, 64], BF16)
          for i in range(2)] if co >= 2 else None
    ys = [bld.dma_slot(f"l1y{i}_h{h2}{hb['_tag']}", "sync") for i in range(2)]
    y_st = {}
    sbf_guard = {}
    prev_acgp = [(ac, 0), (gp, 0)]
    for c in range(8):
        csl = slice(c * 64, (c + 1) * 64)
        vt_rdy[c] = vt_slots[c % 2].fill(
            vt2[c % 2],
            ap3(hb["l1vt"], c * 64 * 2048, [[0, 2], [2048, 64], [1, 2048]]),
            waits=vt_ready + ((mm, vt_guard.get(c - 2, 0)),))
        vt_tile = vt2[c % 2]
        wl = wl2[c % 2]
        wld = _l1_wl_chunk(bld, cm, ar, S, hb, tanh_td, csl, wl, f"c{c}")
        for f in range(FC):
            u = c * FC + f
            bld.scan(bt, wl[f], hb["_zero64_sb"], 0.0, ALU.add, ALU.add,
                     waits=(wld, hb["_z64_rdy"]) + tuple(prev_acgp))
            if co >= 1:
                bld.ts("vector", biasA, bt[:, 63:64], -1.0, LNS, ALU.mult, ALU.add)
                bld.act(At, bt, AF.Exp, bias=biasA[:, 0:1], waits=((dv, dv.n),),
                        inc=ac)
            bld.act(emt, bt, AF.Exp, scale=-1.0, bias=bt[:, 63:64],
                    waits=((dv, dv.n),), inc=ac)
            bld.act(Dcol, bt[:, 63:64], AF.Exp, inc=ac)
            bld.act(ewl, wl[f], AF.Exp, inc=ac)
            ac_e = ac.n
            bld.ts("vector", ewl, ewl, -1.0, 1.0, ALU.mult, ALU.add,
                   waits=((ac, ac_e),))
            bld.tt("vector", ewl, ewl, emt, ALU.mult)
            bld.tt("vector", kt[f % 2], k[f][:, csl], ewl, ALU.mult,
                   waits=((mm, y_st.get(("k", f % 2), 0)),))
            if co >= 1:
                bld.tt("gpsimd", qt[f % 2], q[f][:, csl], At,
                       ALU.mult, waits=((dv, dv.n), (ac, ac_e),
                                        (mm, y_st.get(("q", f % 2), 0))))
            prev_acgp = [(ac, ac_e), (gp, gp.n)]
            bld.ts("vector", S_t[f], S_t[f], Dcol[:, 0:1], 0.0, ALU.mult, ALU.add)
            if co >= 2:
                bld.cp("vector", S_bf[f], S_t[f],
                       waits=((mm, sbf_guard.get(f, 0)),))
            kq_done = ((dv, dv.n), (gp, gp.n))
            if co >= 1:
                bank_p = 0 + (u % 2)
                w0 = bld.use_bank(bank_p, kq_done)
                bld.mm(bld.psum[bank_p][0:64, 0:64], kt[f % 2][0:64, :],
                       qt[f % 2][0:64, :], True, True, waits=w0)
                bld.mm(bld.psum[bank_p][64:128, 0:64], kt[f % 2][64:128, :],
                       qt[f % 2][64:128, :], True, True, inc=mm, tp=(64, 64))
                pm = pm2[u % 2]
                bld.ts("vector", pm, bld.psum[bank_p][0:128, 0:64], 3.0e37,
                       -3.0e37, ALU.min, ALU.max, waits=((mm, mm.n),))
                bld.tt("vector", pm, pm, mask, ALU.mult, waits=(hb["_mask_rdy"],))
                bld.guard_bank(bank_p, dv, dv.n)
                pm_done = (dv, dv.n)
            if co >= 2:
                bank_o = 2 + (u % 2)
                w0 = bld.use_bank(bank_o, (pm_done, vt_rdy[c]))
                bld.mm(bld.psum[bank_o][0:64, 0:64],
                       vt_tile[0:64, (2 * f) * 64:(2 * f) * 64 + 64],
                       pm[0:64, 0:64], True, False, waits=w0)
                bld.mm(bld.psum[bank_o][0:64, 0:64], S_bf[f][0:64, :],
                       qt[f % 2][0:64, :], False, True)
                bld.mm(bld.psum[bank_o][64:128, 0:64],
                       vt_tile[64:128, (2 * f + 1) * 64:(2 * f + 1) * 64 + 64],
                       pm[64:128, 0:64], True, False, tp=(64, 64))
                bld.mm(bld.psum[bank_o][64:128, 0:64], S_bf[f][64:128, :],
                       qt[f % 2][64:128, :], False, True, inc=mm, tp=(64, 64))
                mm_o = mm.n
                sbf_guard[f] = mm_o
                ypt = yp[u % 2]
                stp = y_st.get(("y", u % 2), None)
                ww = ((mm, mm_o),) + ((stp,) if stp else ())
                bld.cp("vector", ypt[:], bld.psum[bank_o][0:128, 0:64], waits=ww)
                bld.guard_bank(bank_o, dv, dv.n)
            if co >= 3:
                y_st[("y", u % 2)] = ys[u % 2].fill(
                    ap3(y_hbm, (2 * f) * 64 * 512 + c * 64,
                        [[512, 128], [1, 64]]),
                    ap3(ypt, 0, [[64, 128], [1, 64]]),
                    waits=((dv, dv.n),))
            bank_t = 6 + (u % 2)
            w0 = bld.use_bank(bank_t, ((dv, dv.n), cm.ready))
            bld.transp(bld.psum[bank_t][0:64, 0:64].bitcast(BF16),
                       kt[f % 2][:], cm.ident_bf[:], waits=w0, inc=mm)
            bld.cp("vector", ktT[f % 2], bld.psum[bank_t][0:64, 0:64].bitcast(BF16),
                   waits=((mm, mm.n),))
            bld.guard_bank(bank_t, dv, dv.n)
            ktT_done = (dv, dv.n)
            bank_kv = 4 + (u % 2)
            w0 = bld.use_bank(bank_kv, (ktT_done, vt_rdy[c]))
            bld.mm(bld.psum[bank_kv][0:64, 0:64], ktT[f % 2][:, 0:64],
                   vt_tile[0:64, (2 * f) * 64:(2 * f) * 64 + 64], True, True,
                   waits=w0)
            bld.emit("tensor", lambda e, o=bld.psum[bank_kv][64:128, 0:64],
                     l=ktT[f % 2][:, 64:128],
                     r=vt_tile[0:64, (2 * f + 1) * 64:(2 * f + 1) * 64 + 64]:
                     e.matmul(o, l, r, start=True, stop=True, tile_position=(0, 64)),
                     inc=mm)
            mm_kv = mm.n
            bld.tt("vector", S_t[f], S_t[f], bld.psum[bank_kv][0:128, 0:64], ALU.add,
                   waits=((mm, mm_kv),))
            bld.guard_bank(bank_kv, dv, dv.n)
            y_st[("k", f % 2)] = mm.n
            y_st[("q", f % 2)] = mm.n
            vt_guard[c] = mm.n
            if Dacc is not None:
                bld.ts("vector", Dacc[:, f:f + 1], Dacc[:, f:f + 1], Dcol[:, 0:1],
                       0.0, ALU.mult, ALU.add)
    return (dv, dv.n), ys


def _l1_prep_smalls(bld, cm, ar, S, hb, tag=""):
    hb["_tag"] = tag
    maa = ar.alloc(128, 96, F32)
    s1 = bld.dma_slot("l1maa" + tag, "sync")
    hb["_maa_rdy"] = s1.fill(maa, hb["l1_maa"][:, :])
    hb["_maa_sb"] = maa
    qkb = ar.alloc(128, 32, F32)
    s2 = bld.dma_slot("l1qkb" + tag, "sync")
    hb["_qkb_rdy"] = s2.fill(qkb, hb["l1_qkb"][:, :])
    hb["_qkb_sb"] = qkb
    vb = ar.alloc(1, 2048, BF16)
    s3 = bld.dma_slot("l1vb" + tag, "gpsimd")
    hb["_vb_rdy"] = s3.fill(vb, hb["l1_vb"][:, :])
    hb["_vb_sb"] = vb
    td2 = ar.alloc(64, 16 * 128, BF16)
    s4 = bld.dma_slot("l1td2" + tag, "gpsimd")
    hb["_td2_rdy"] = s4.fill(td2, ap3(hb["l1_td2"], 0, [[2048, 64], [1, 2048]]))
    hb["_td2_sb"] = td2
    swp = ar.alloc(128, 128, BF16)
    s5 = bld.dma_slot("l1swp" + tag, "gpsimd")
    hb["_swp_rdy"] = s5.fill(swp, hb["c_swap"][:, :])
    hb["_swp_sb"] = swp
    mask = ar.alloc(128, 64, BF16)
    s6 = bld.dma_slot("l1mask" + tag, "sync")
    hb["_mask_rdy"] = s6.fill(mask, hb["c_mask64"][:, :])
    hb["_mask64_sb"] = mask
    z64 = ar.alloc(128, 64, F32)
    bld.mset("vector", z64, 0.0)
    hb["_zero64_sb"] = z64
    hb["_z64_rdy"] = (S["dv"], S["dv"].n)


def emit_l1bnd(bld, cm, ar, S, hb, x_tiles, x_ready):
    mm, dv, ac = S["mm"], S["dv"], S["ac"]
    xc = ar.alloc(128, 32, F32)
    x2 = ar.alloc(128, 32, F32)
    rows = bld.nc_sbuf("l1b_rows", [1, 32], F32)
    st_r = bld.nc_sbuf("l1b_st", [1, 8], F32)
    for s in range(2):
        for c in range(FC):
            bld.cp("vector", xc[:, s * 16 + c:s * 16 + c + 1],
                   x_tiles[c][:, s * 512 + 511:s * 512 + 512], waits=(x_ready,))
    bld.tt("vector", x2, xc, xc, ALU.mult)
    w0 = bld.use_bank(0, ((dv, dv.n), cm.ready))
    bld.mm(bld.psum[0][0:1, 0:32], cm.ones_col_f[:], xc, True, True, waits=w0)
    bld.mm(bld.psum[0][0:1, 32:64], cm.ones_col_f[:], x2, True, True, inc=mm)
    bld.cp("vector", rows[0:1, :], bld.psum[0][0:1, 0:32], waits=((mm, mm.n),))
    bld.guard_bank(0, dv, dv.n)
    for s in range(2):
        bld.red(st_r[0:1, s:s + 1], ap3(rows, s * 16, [[32, 1], [1, 16]]), ALU.add)
    bld.cp("vector", rows[0:1, 0:32], bld.psum[0][0:1, 32:64])
    for s in range(2):
        bld.red(st_r[0:1, 2 + s:3 + s], ap3(rows, s * 16, [[32, 1], [1, 16]]), ALU.add)
    bld.guard_bank(0, dv, dv.n)
    bld.ts("vector", st_r[0:1, 0:2], st_r[0:1, 0:2], 1.0 / D, 0.0, ALU.mult, ALU.add)
    bld.ts("vector", st_r[0:1, 2:4], st_r[0:1, 2:4], 1.0 / D, 0.0, ALU.mult, ALU.add)
    bld.tt("vector", st_r[0:1, 4:6], st_r[0:1, 0:2], st_r[0:1, 0:2], ALU.mult)
    bld.tt("vector", st_r[0:1, 2:4], st_r[0:1, 2:4], st_r[0:1, 4:6], ALU.subtract)
    eps_ap, eps_rdy = cm.constf(1e-5)
    bld.act(st_r[0:1, 2:4], st_r[0:1, 2:4], AF.Ln, bias=eps_ap[0:1, :],
            waits=((dv, dv.n), eps_rdy), inc=ac)
    bld.act(st_r[0:1, 2:4], st_r[0:1, 2:4], AF.Exp, scale=-0.5, inc=ac)
    mucol = ar.alloc(128, 2, F32)
    rscol = ar.alloc(128, 2, F32)
    for s in range(2):
        w0 = bld.use_bank(1, ((ac, ac.n), (dv, dv.n)))
        bld.mm(bld.psum[1][:, 0:1], cm.ones_row_f[:], st_r[0:1, s:s + 1], True, True,
               waits=w0, inc=mm)
        bld.cp("vector", mucol[:, s:s + 1], bld.psum[1][:, 0:1], waits=((mm, mm.n),))
        bld.guard_bank(1, dv, dv.n)
        w0 = bld.use_bank(1, ((dv, dv.n),))
        bld.mm(bld.psum[1][:, 0:1], cm.ones_row_f[:], st_r[0:1, 2 + s:3 + s], True,
               True, waits=w0, inc=mm)
        bld.cp("vector", rscol[:, s:s + 1], bld.psum[1][:, 0:1], waits=((mm, mm.n),))
        bld.guard_bank(1, dv, dv.n)
    out = ar.alloc(128, 32, F32)
    for s in range(2):
        bld.ts("vector", out[:, s * 16:(s + 1) * 16], xc[:, s * 16:(s + 1) * 16],
               mucol[:, s:s + 1], rscol[:, s:s + 1], ALU.subtract, ALU.mult)
    so = bld.dma_slot("l1bnd_o", "sync")
    so.fill(hb["l1bnd"][:, :], out, waits=((dv, dv.n),))
    return (dv, dv.n)


def emit_l1agg(bld, cm, ar, S, hb, x_tiles, x_ready):
    mm, dv, ac = S["mm"], S["dv"], S["ac"]
    mark0 = ar.mark()
    _l1_prep_smalls(bld, cm, ar, S, hb)
    S_t = [ar.alloc(128, 64, F32) for _ in range(FC)]
    Dacc = ar.alloc(128, 16, F32)
    done = x_ready
    bndp = ar.alloc(128, 32, F32)
    sq2 = bld.dma_slot("l1bndp", "sync")
    hb["_bnd_rdy"] = sq2.fill(bndp, hb["l1bnd_prev"][:, :])
    store_done = ()
    for h2 in range(2):
        mark = ar.mark()
        outs = {"xn": [ar.alloc(128, 512, BF16) for _ in range(FC)],
                "q": None,
                "k": [ar.alloc(128, 512, BF16) for _ in range(FC)],
                "tanh_td": ar.alloc(64, 512, BF16)}
        outs["xxx_bufs"] = outs["k"]
        bcol = bndp[:, h2 * 16:(h2 + 1) * 16]
        mark_pa = ar.mark()
        pa_done = _l1_phase_a(bld, cm, ar, S, hb, x_tiles, done, h2, bcol, outs,
                              skip_q=True)
        ar.reset(mark_pa)
        fw = (pa_done[0], pa_done[1]) + pa_done[2]
        bld.emit("gpsimd", None, waits=fw)
        bld.emit("sync", None, waits=fw)
        for f in range(FC):
            bld.mset("vector", S_t[f], 0.0, waits=store_done)
        bld.mset("vector", Dacc, 1.0)
        ch_done, _ = _l1_chunks(bld, cm, ar, S, hb, h2, None, outs["k"],
                                pa_done[2], outs["tanh_td"], S_t, None, False,
                                Dacc=Dacc)
        so1 = bld.dma_slot(f"l1aggD_{h2}", "sync")
        q1 = so1.fill(ap3(hb["l1D"], h2 * 16, [[32, 128], [1, 16]]), Dacc,
                      waits=(ch_done,))
        qs = [q1]
        for f in range(FC):
            so2 = bld.dma_slot(f"l1aggKV_{h2}_{f}", "sync")
            qs.append(so2.fill(
                ap3(hb["l1KV"], (h2 * 16 + f) * 128 * 64, [[64, 128], [1, 64]]),
                S_t[f], waits=(ch_done,)))
        store_done = tuple(qs)
        bld.wait_engines(store_done)
        done = ch_done
        ar.reset(mark)
        bld.emit("gpsimd", None, waits=(ch_done,))
        bld.emit("sync", None, waits=(ch_done,))
    ar.reset(mark0)
    return done


def emit_l1full(bld, cm, ar, S, hb, x_tiles, x_ready):
    mm, dv, ac, gp = S["mm"], S["dv"], S["ac"], S["gp"]
    cut = hb.get("_cut", 5)
    mark0 = ar.mark()
    _l1_prep_smalls(bld, cm, ar, S, hb)
    S_A = [ar.alloc(128, 64, F32) for _ in range(FC)]
    S_B = [ar.alloc(128, 64, F32) for _ in range(FC)]
    bndp = ar.alloc(128, 32, F32)
    sq2 = bld.dma_slot("l1bndp", "sync")
    bnd_rdy = sq2.fill(bndp, hb["l1bnd_prev"][:, :])
    selab = ar.alloc(128, 32, F32)
    sq3 = bld.dma_slot("l1sel", "sync")
    qsel = sq3.fill(selab, hb["l1sel"][:, :])
    mark_st = ar.mark()
    S_run = [ar.alloc(128, 64, F32) for _ in range(FC)]
    sdiff = ar.alloc(128, 64, F32)
    dall = ar.alloc(128, 256, F32)
    sd = bld.dma_slot("l1dall", "sync")
    qd = sd.fill(dall, hb["l1D_all"][:, :])
    kvl = [ar.alloc(128, FC * 64, F32) for _ in range(2)]
    skv = [bld.dma_slot(f"l1kv{i}", "sync") for i in range(2)]
    for f in range(FC):
        bld.mset("vector", S_run[f], 0.0)
        bld.mset("vector", S_A[f], 0.0)
        bld.mset("vector", S_B[f], 0.0)
    ldg = {}
    for s in range(NSTRIP - 1):
        buf = kvl[s % 2]
        qv = skv[s % 2].fill(buf, ap3(hb["l1KV_all"], s * FC * 128 * 64,
                                      [[64, 128], [128 * 64, FC], [1, 64]]),
                             waits=((dv, ldg.get(s - 2, 0)),))
        for f in range(FC):
            bld.ts("vector", S_run[f], S_run[f], dall[:, s * 16 + f:s * 16 + f + 1],
                   0.0, ALU.mult, ALU.add, waits=(qv, qd))
            bld.tt("vector", S_run[f], S_run[f], buf[:, f * 64:(f + 1) * 64], ALU.add)
            bld.tt("vector", sdiff, S_run[f], S_A[f], ALU.subtract, waits=(qsel,))
            bld.stt("vector", S_A[f], sdiff, selab[:, s:s + 1], S_A[f],
                    ALU.mult, ALU.add)
            bld.tt("vector", sdiff, S_run[f], S_B[f], ALU.subtract)
            bld.stt("vector", S_B[f], sdiff, selab[:, 16 + s:16 + s + 1], S_B[f],
                    ALU.mult, ALU.add)
        ldg[s] = dv.n
    states_done = (dv, dv.n)
    ar.reset(mark_st)
    bld.emit("sync", None, waits=(states_done,))
    bld.emit("gpsimd", None, waits=(states_done,))
    S_t = [ar.alloc(128, 64, F32) for _ in range(FC)]
    S_bf = [ar.alloc(128, 64, BF16) for _ in range(FC)]
    done = x_ready
    for h2 in range(2):
        sl = slice(h2 * 512, (h2 + 1) * 512)
        mark = ar.mark()
        outs = {"xn": [ar.alloc(128, 512, BF16) for _ in range(FC)],
                "q": [ar.alloc(128, 512, BF16) for _ in range(FC)],
                "k": [ar.alloc(128, 512, BF16) for _ in range(FC)],
                "tanh_td": ar.alloc(64, 512, BF16)}
        outs["xxx_bufs"] = outs["q"]
        outs["w1_bufs"] = outs["k"][0:4]
        hb["_bnd_rdy"] = bnd_rdy
        bcol = bndp[:, h2 * 16:(h2 + 1) * 16]
        mark_pa = ar.mark()
        pa_done = _l1_phase_a(bld, cm, ar, S, hb, x_tiles, done, h2, bcol, outs)
        ar.reset(mark_pa)
        fw = (pa_done[0], pa_done[1]) + pa_done[2]
        bld.emit("gpsimd", None, waits=fw)
        bld.emit("sync", None, waits=fw)
        S_in = S_A if h2 == 0 else S_B
        for f in range(FC):
            bld.cp("vector", S_t[f], S_in[f], waits=(states_done,))
        ch_done, y_slots = _l1_chunks(bld, cm, ar, S, hb, h2, outs["q"],
                                      outs["k"], pa_done[2], outs["tanh_td"],
                                      S_t, S_bf, min(3, max(0, cut - 1)),
                                      y_hbm=hb["l1y"])
        ysems = [(s_.sem, s_.sem.n) for s_ in y_slots]
        bld.wait_engines(tuple(ysems) + (ch_done,))
        ar.reset(mark)
        bld.emit("sync", None, waits=tuple(ysems) + (ch_done,))
        bld.emit("gpsimd", None, waits=tuple(ysems) + (ch_done,))
        if cut < 5:
            done = ch_done
            continue
        mark_gn = ar.mark()
        yt = [ar.alloc(128, 512, BF16) for _ in range(FC)]
        y2 = [ar.alloc(128, 512, BF16) for _ in range(2)]
        lnw = ar.alloc(128, 32, F32)
        sgl = bld.dma_slot(f"l1lng_h{h2}", "sync")
        qlg = sgl.fill(lnw, hb["l1_lng"][:, :])
        syl = [bld.dma_slot(f"l1yl{i}_h{h2}", "sync") for i in range(4)]
        yld = []
        for kc in range(FC):
            sl_ = syl[kc % 4]
            w = (((sl_.sem, sl_.sem.n),) if kc >= 4 else tuple(ysems))
            qv = sl_.fill(yt[kc], ap3(hb["l1y"], kc * 128 * 512,
                                      [[512, 128], [1, 512]]), waits=w)
            yld.append(qv)
        sq_ac = {}
        for kc in range(FC):
            bld.act(y2[kc % 2], yt[kc], AF.Square,
                    waits=(yld[kc], (mm, sq_ac.get(kc - 2, 0))), inc=ac)
            w0 = bld.use_bank(0, (yld[kc], cm.ready)) if kc == 0 else (yld[kc],)
            bld.mm(bld.psum[0][0:1, 0:512], cm.ones_col_bf[:], yt[kc],
                   kc == 0, kc == FC - 1, waits=w0,
                   inc=mm if kc == FC - 1 else None)
            w1 = bld.use_bank(1, ((ac, ac.n),)) if kc == 0 else ((ac, ac.n),)
            bld.mm(bld.psum[1][0:1, 0:512], cm.ones_col_bf[:], y2[kc % 2],
                   kc == 0, kc == FC - 1, waits=w1, inc=mm)
            sq_ac[kc] = mm.n
        mm_stats = mm.n
        rows = ar.alloc(1, 2048, F32)
        mu, var, rstd, nmur = (rows[0:1, i * 512:(i + 1) * 512] for i in range(4))
        bld.ts("vector", mu, bld.psum[0][0:1, 0:512], 1.0 / D, 0.0, ALU.mult,
               ALU.add, waits=((mm, mm_stats),))
        bld.ts("vector", var, bld.psum[1][0:1, 0:512], 1.0 / D, 0.0, ALU.mult,
               ALU.add)
        bld.guard_bank(0, dv, dv.n)
        bld.guard_bank(1, dv, dv.n)
        bld.tt("vector", nmur, mu, mu, ALU.mult)
        bld.tt("vector", var, var, nmur, ALU.subtract)
        eps_ap, eps_rdy = cm.constf(1e-5)
        bld.act(rstd, var, AF.Ln, bias=eps_ap[0:1, :], waits=((dv, dv.n), eps_rdy),
                inc=ac)
        bld.act(rstd, rstd, AF.Exp, scale=-0.5, inc=ac)
        bld.tt("vector", nmur, mu, rstd, ALU.mult, waits=((ac, ac.n),))
        bld.ts("vector", nmur, nmur, -1.0, 0.0, ALU.mult, ALU.add)
        dvr = dv.n
        w0 = bld.use_bank(0, ((dv, dvr),))
        bld.mm(bld.psum[0][:, 0:512], cm.ones_row_f[:], rstd, True, True,
               waits=w0, inc=mm)
        w1 = bld.use_bank(1, ((dv, dvr),))
        bld.mm(bld.psum[1][:, 0:512], cm.ones_row_f[:], nmur, True, True,
               waits=w1, inc=mm)
        mm_bc = mm.n
        yn = [ar.alloc(128, 512, BF16) for _ in range(FC)]
        t64 = ar.alloc(128, 512, F32)
        for kc in range(FC):
            bld.tt("vector", t64, yt[kc], bld.psum[0][:, 0:512], ALU.mult,
                   waits=((mm, mm_bc),))
            bld.tt("vector", t64, t64, bld.psum[1][:, 0:512], ALU.add)
            bld.ts("vector", yn[kc], t64, lnw[:, kc:kc + 1],
                   lnw[:, 16 + kc:16 + kc + 1], ALU.mult, ALU.add, waits=(qlg,))
        bld.guard_bank(0, dv, dv.n)
        bld.guard_bank(1, dv, dv.n)
        yn_done = (dv, dv.n)
        wos = [ar.alloc(128, 2048, BF16) for _ in range(2)]
        slot_o = [bld.dma_slot(f"l1o{i}_h{h2}", "gpsimd") for i in range(2)]
        mm_prev2 = {-2: mm.n, -1: mm.n}
        for oc in range(FC):
            buf = wos[oc % 2]
            qv = slot_o[oc % 2].fill(buf, ap3(hb["l1_ow"], oc * 128 * 2048,
                                              [[2048, 128], [1, 2048]]),
                                     waits=((mm, mm_prev2[oc - 2]),))
            bank = 2 + (oc % 2)
            for kc in range(FC):
                w0 = bld.use_bank(bank, (qv, yn_done)) if kc == 0 else ()
                bld.mm(bld.psum[bank][:, 0:512], buf[:, kc * 128:(kc + 1) * 128],
                       yn[kc], kc == 0, kc == FC - 1, waits=w0,
                       inc=mm if kc == FC - 1 else None)
            mm_prev2[oc] = mm.n
            bld.tt("vector", x_tiles[oc][:, sl], x_tiles[oc][:, sl],
                   bld.psum[bank][:, 0:512], ALU.add, waits=((mm, mm.n),))
            bld.guard_bank(bank, dv, dv.n)
        done = (dv, dv.n)
        ar.reset(mark_gn)
        bld.emit("gpsimd", None, waits=(done,))
        bld.emit("sync", None, waits=(done,))
    ar.reset(mark0)
    return done


def host_prep_l1attn(inputs):
    g = lambda n: np.asarray(inputs["L1_attn_" + n])
    maa = np.stack([col_layout(g(n)) for n in
                    ("maa_x", "maa_r", "maa_k", "maa_v", "maa_w", "time_decay")],
                   axis=1).reshape(128, 96)
    qkb = np.concatenate([col_layout(g("q_b")), col_layout(g("k_b"))], axis=1)
    lng = np.concatenate([col_layout(g("ln_g")), col_layout(g("ln_b"))], axis=1)
    mask64 = np.triu(np.ones((64, 64), np.float32))
    mask64 = np.concatenate([mask64, mask64], axis=0).astype(BF)
    swap = np.zeros((128, 128), np.float32)
    for i in range(128):
        swap[i, (i + 64) % 128] = 1.0
    return {
        "l1_maa": maa,
        "l1_qkb": qkb,
        "l1_vb": np.ascontiguousarray(g("v_b").reshape(1, 2048)),
        "l1_w1t": tile_lhsT(np.ascontiguousarray(g("maa_w1"))),
        "l1_w2": np.ascontiguousarray(g("maa_w2")),
        "l1_td1": np.ascontiguousarray(g("td_w1")),
        "l1_td2": np.ascontiguousarray(g("td_w2")),
        "l1_qw": tile_lhsT(np.ascontiguousarray(g("q_w"))),
        "l1_kw": tile_lhsT(np.ascontiguousarray(g("k_w"))),
        "l1_vwp": np.ascontiguousarray(g("v_w")),
        "l1_ow": tile_lhsT(np.ascontiguousarray(g("o_w"))),
        "l1_lng": np.ascontiguousarray(lng),
        "c_mask64": mask64,
        "c_swap": swap,
    }


L1_WEIGHT_INS = [
    ("l1_maa", (128, 96), F32), ("l1_qkb", (128, 32), F32),
    ("l1_vb", (1, 2048), F32), ("l1_w1t", (1, 128, 16, 128), F32),
    ("l1_w2", (4, 32, 2048), F32), ("l1_td1", (2048, 64), F32),
    ("l1_td2", (64, 2048), F32), ("l1_kw", (16, 128, 16, 128), F32),
    ("l1_vwp", (2048, 2048), F32),
    ("c_mask64", (128, 64), BF16), ("c_swap", (128, 128), F32),
]
L1_FULL_EXTRA = [
    ("l1_qw", (16, 128, 16, 128), F32), ("l1_ow", (16, 128, 16, 128), F32),
    ("l1_lng", (128, 32), F32),
]


def build_l1bnd_program():
    return build_stage_program("l1bnd", [], emit_l1bnd,
                               extra_outs=[("l1bnd", (128, 32), F32)])


def build_l1agg_program():
    def em(bld, cm, ar, S, hb, x_tiles, x_ready):
        hb["l1vt"] = bld.nc.dram_tensor("l1vt", [512, 2048], BF16,
                                        kind="Internal").ap()
        return emit_l1agg(bld, cm, ar, S, hb, x_tiles, x_ready)
    ins = L1_WEIGHT_INS + [("l1bnd_prev", (128, 32), F32)]
    outs = [("l1D", (128, 32), F32), ("l1KV", (2, 16, 128, 64), F32)]
    return build_stage_program("l1agg", ins, em, arena_kb=140, extra_outs=outs)


def build_l1full_program():
    def em(bld, cm, ar, S, hb, x_tiles, x_ready):
        hb["l1y"] = bld.nc.dram_tensor("l1y", [2048, 512], BF16,
                                       kind="Internal").ap()
        hb["l1vt"] = bld.nc.dram_tensor("l1vt", [512, 2048], BF16,
                                        kind="Internal").ap()
        return emit_l1full(bld, cm, ar, S, hb, x_tiles, x_ready)
    ins = (L1_WEIGHT_INS + L1_FULL_EXTRA +
           [("l1bnd_prev", (128, 32), F32), ("l1sel", (128, 32), F32),
            ("l1D_all", (128, 256), F32), ("l1KV_all", (16, 16, 128, 64), F32)])
    return build_stage_program("l1full", ins, em, arena_kb=140)


def host_l1_percore(bnd_all):
    outs = []
    for c in range(NCORES):
        sA, sB = strips_of(c)
        bp = np.zeros((128, 32), np.float32)
        if sA > 0:
            bp[:, 0:16] = bnd_all[:, (sA - 1) * 16:sA * 16]
        if sB > 0:
            bp[:, 16:32] = bnd_all[:, (sB - 1) * 16:sB * 16]
        sel = np.zeros((128, 32), np.float32)
        if sA > 0:
            sel[:, sA - 1] = 1.0
        if sB > 0:
            sel[:, 16 + sB - 1] = 1.0
        outs.append((bp, sel))
    return outs


def build_l1full_cut_program(core, cut):
    def em(bld, cm, ar, S, hb, x_tiles, x_ready):
        hb["_cut"] = cut
        hb["l1y"] = bld.nc.dram_tensor("l1y", [2048, 512], BF16,
                                       kind="Internal").ap()
        hb["l1vt"] = bld.nc.dram_tensor("l1vt", [512, 2048], BF16,
                                        kind="Internal").ap()
        return emit_l1full(bld, cm, ar, S, hb, x_tiles, x_ready)
    ins = (L1_WEIGHT_INS + L1_FULL_EXTRA +
           [("l1bnd_prev", (128, 32), F32), ("l1sel", (128, 32), F32),
            ("l1D_all", (128, 256), F32), ("l1KV_all", (16, 16, 128, 64), F32)])
    return build_stage_program("l1full", ins, em, arena_kb=140)


def host_assemble_l1(bnds=None, Ds=None, KVs=None):
    out = []
    for per in (bnds, Ds):
        if per is None:
            out.append(None)
            continue
        allx = np.zeros((128, 256), np.float32)
        for c in range(NCORES):
            a, b = strips_of(c)
            allx[:, a * 16:(a + 1) * 16] = per[c][:, 0:16]
            allx[:, b * 16:(b + 1) * 16] = per[c][:, 16:32]
        out.append(allx)
    if KVs is None:
        out.append(None)
    else:
        kv_all = np.zeros((16, 16, 128, 64), np.float32)
        for c in range(NCORES):
            a, b = strips_of(c)
            kv_all[a] = KVs[c][0]
            kv_all[b] = KVs[c][1]
        out.append(kv_all)
    return out



def build_l1prepa_program():
    em = lambda bld, cm, ar, S, hb, xt, xr: emit_l1prep_a(bld, cm, ar, S, hb)
    return build_stage_program("l1prepa",
                               [("qsrc", (128, NQ_COLS_PER_CORE), F32)],
                               em, with_x=False,
                               extra_outs=[("g_part", (1, 3), F32)])


def build_l1prepb_program():
    em = lambda bld, cm, ar, S, hb, xt, xr: emit_l1prep_b(bld, cm, ar, S, hb)
    return build_stage_program("l1prepb",
                               [("qsrc", (128, NQ_COLS_PER_CORE), F32),
                                ("g_all", (8, 3), F32)],
                               em, with_x=False,
                               extra_outs=[("q_loc", (128, NQ_COLS_PER_CORE),
                                            BF16)])


def build_l1ffn_program(shapes):
    return build_stage_program("l1ffn", [("q_w1", shapes[0], BF16),
                                         ("q_w2", shapes[1], BF16),
                                         ("q_w3", shapes[2], BF16)], emit_l1ffn)


def build_l3ffn_program(hp):
    return build_stage_program("l3ffn", [(k, hp[k].shape, F32) for k in
                                         ("l3_up", "l3_dn", "l3_ex")], emit_l3ffn)


def build_l2ffn_program(hp):
    return build_stage_program("l2ffn", [(k, hp[k].shape, F32) for k in
                                         ("l2_up", "l2_dn", "l2_rt")], emit_l2ffn)


def build_l3attn_program(hp):
    return build_stage_program("l3attn", [(k, v.shape, F32) for k, v in hp.items()],
                               emit_l3attn, arena_kb=142)


L2A_OUTS = [("l2kv", (2, 16, 128, 128), F32), ("l2z", (128, 32), F32),
            ("l2qs", (128, 16), F32)]


def build_l2attn_a_program(hp):
    ins_a = [("l2a_q", hp["l2a_q"].shape, F32), ("l2a_kp", (2048, 2048), F32),
             ("l2a_vp", (2048, 2048), F32)]
    return build_stage_program("l2attn_a", ins_a, emit_l2attn_a, arena_kb=140,
                               extra_outs=L2A_OUTS)


L2B_KEYS = ("l2a_qt", "l2a_kt", "l2a_ot", "l2a_kp", "l2a_vp", "l2mem",
            "l2mn_hn", "l2mn_nh", "l2gate", "c_mask128")


def build_l2attn_b_program(hp):
    ins_b = [(k, hp[k].shape, F32 if hp[k].dtype == np.float32 else BF16)
             for k in L2B_KEYS]
    ins_b += [("kv_all", (16, 16, 128, 128), F32), ("z_all", (128, 256), F32),
              ("qs_all", (128, 128), F32), ("l2sel", (128, 32), F32)]
    return build_stage_program("l2attn_b", ins_b, emit_l2attn_b, arena_kb=140)


def host_l2_sel():
    outs = []
    for c in range(NCORES):
        sA, sB = strips_of(c)
        sel = np.zeros((128, 32), np.float32)
        sel[:, 0:sA] = 1.0
        sel[:, 16:16 + sB] = 1.0
        outs.append(sel)
    return outs


def kernel(**inputs):
    from concourse.bass_utils import run_bass_kernel_spmd
    consts = const_arrays()
    cores = list(range(NCORES))
    x0 = np.asarray(inputs["L0_attn_x"])
    xT = np.ascontiguousarray(x0.reshape(T, D).T)
    xs = split_x(xT)

    def launch(nc, maps, outs):
        res = run_bass_kernel_spmd(nc, maps, core_ids=cores)
        return [{k: np.asarray(r[k]) for k in outs} for r in res.results]

    def launch1(nc, m, outs, core):
        r = run_bass_kernel_spmd(nc, [m], core_ids=[core]).results[0]
        return {k: np.asarray(r[k]) for k in outs}

    def xstage(nc, extra_maps):
        nonlocal xs
        maps = [dict(consts, x_in=xs[c], **extra_maps[c]) for c in cores]
        res = launch(nc, maps, ["x_out"])
        xs = [np.ascontiguousarray(r["x_out"].reshape(D, TLOC)) for r in res]

    def xstage_percore(build_fn, extra_maps):
        nonlocal xs
        new = []
        for c in cores:
            r = launch1(build_fn(c), dict(consts, x_in=xs[c], **extra_maps[c]),
                        ["x_out"], c)
            new.append(np.ascontiguousarray(r["x_out"].reshape(D, TLOC)))
        xs = new

    per0 = host_prep_l0ffn(inputs)
    res = launch(build_deq_program(), [dict(consts, **per0[c]) for c in cores],
                 ["deq_gu_loc", "deq_dn_loc"])
    deq_gu = np.concatenate(
        [r["deq_gu_loc"].reshape(DEQ_GU_PER_CORE, 128, 16, 128) for r in res])
    deq_dn = np.concatenate(
        [r["deq_dn_loc"].reshape(DEQ_DN_PER_CORE, 128, 44, 128) for r in res])

    per1 = host_prep_l1ffn(inputs)
    res = launch(build_l1prepa_program(), [dict(consts, **per1[c]) for c in cores],
                 ["g_part"])
    g_all = np.ascontiguousarray(
        np.concatenate([r["g_part"].reshape(1, 3) for r in res], axis=0))
    res = launch(build_l1prepb_program(),
                 [dict(consts, g_all=g_all, **per1[c]) for c in cores], ["q_loc"])
    qlocs = [r["q_loc"].reshape(128, NQ_COLS_PER_CORE) for r in res]
    q1, q2, q3 = host_assemble_l1q(qlocs)

    nc_l0f = build_l0ffn_program()
    ffn0 = [dict(deq_gu=deq_gu, deq_dn=deq_dn) for _ in cores]
    xstage(nc_l0f, ffn0)

    sh0, per0a = host_prep_l0attn(inputs)
    sh_a = {k: sh0[k] for k in ("l0a_q", "l0a_k", "l0a_vt", "c_swap")}
    maps = [dict(consts, x_in=xs[c], **sh_a, **per0a[c]) for c in cores]
    res = launch(build_l0attn_a_program(), maps, ["q_rope", "k_rope", "vt"])
    qropes = [r["q_rope"].reshape(D, TLOC) for r in res]
    K_full = merge_strips([r["k_rope"].reshape(D, TLOC) for r in res], axis=1)
    VT_full = merge_strips([r["vt"].reshape(TLOC, D) for r in res], axis=0)
    masks = l0attn_masks()
    extra = []
    for c in cores:
        ks, vs = l0_streams_dense(c, K_full, VT_full)
        extra.append(dict(q_rope=qropes[c], k_stream=ks, vt_stream=vs,
                          c_masks=masks, l0a_o=sh0["l0a_o"]))
    xstage(build_l0attn_bu_program(), extra)

    xstage(nc_l0f, ffn0)

    nc_l1f = build_l1ffn_program((q1.shape, q2.shape, q3.shape))
    ffn1 = [dict(q_w1=q1, q_w2=q2, q_w3=q3) for _ in cores]
    xstage(nc_l1f, ffn1)

    hp1 = host_prep_l1attn(inputs)
    wmap = {k: hp1[k] for k, _, _ in L1_WEIGHT_INS}
    wmap_f = dict(wmap, **{k: hp1[k] for k, _, _ in L1_FULL_EXTRA})
    res = launch(build_l1bnd_program(),
                 [dict(consts, x_in=xs[c]) for c in cores], ["l1bnd"])
    bnd_all = host_assemble_l1(
        bnds=[r["l1bnd"].reshape(128, 32) for r in res])[0]
    pc1 = host_l1_percore(bnd_all)
    res = launch(build_l1agg_program(),
                 [dict(consts, x_in=xs[c], l1bnd_prev=pc1[c][0], **wmap)
                  for c in cores], ["l1D", "l1KV"])
    Ds = [r["l1D"].reshape(128, 32) for r in res]
    KVs = [r["l1KV"].reshape(2, 16, 128, 64) for r in res]
    _, D_all, KV_all = host_assemble_l1(Ds=Ds, KVs=KVs)
    extra = [dict(l1bnd_prev=pc1[c][0], l1sel=pc1[c][1], l1D_all=D_all,
                  l1KV_all=KV_all, **wmap_f) for c in cores]
    xstage(build_l1full_program(), extra)

    xstage(nc_l1f, ffn1)

    hp2f = host_prep_l2ffn(inputs)
    nc_l2f = build_l2ffn_program(hp2f)
    ffn2 = [hp2f for _ in cores]
    xstage(nc_l2f, ffn2)

    hp2 = host_prep_l2attn(inputs)
    hpa = {k: hp2[k] for k in ("l2a_q", "l2a_kp", "l2a_vp")}
    maps = [dict(consts, x_in=xs[c], **hpa) for c in cores]
    res = launch(build_l2attn_a_program(hp2), maps, [k for k, _, _ in L2A_OUTS])
    kv_all = np.zeros((16, 16, 128, 128), np.float32)
    z_all = np.zeros((128, 256), np.float32)
    for c in cores:
        a2, b2 = strips_of(c)
        kv = res[c]["l2kv"].reshape(2, 16, 128, 128)
        z2 = res[c]["l2z"].reshape(128, 32)
        kv_all[a2], kv_all[b2] = kv[0], kv[1]
        z_all[:, a2 * 16:(a2 + 1) * 16] = z2[:, 0:16]
        z_all[:, b2 * 16:(b2 + 1) * 16] = z2[:, 16:32]
    qs_all = np.concatenate([res[c]["l2qs"].reshape(128, 16) for c in cores],
                            axis=1)
    sels2 = host_l2_sel()
    extra = [dict(kv_all=kv_all, z_all=z_all, qs_all=qs_all, l2sel=sels2[c],
                  **{k: hp2[k] for k in L2B_KEYS}) for c in cores]
    xstage(build_l2attn_b_program(hp2), extra)

    xstage(nc_l2f, ffn2)

    hp3f = host_prep_l3ffn(inputs)
    nc_l3f = build_l3ffn_program(hp3f)
    ffn3 = [hp3f for _ in cores]
    xstage(nc_l3f, ffn3)

    hp3 = host_prep_l3attn(inputs)
    xstage(build_l3attn_program(hp3), [hp3 for _ in cores])

    xstage(nc_l3f, ffn3)

    out = merge_x(xs)
    return np.ascontiguousarray(out.T).reshape(1, T, D)
